# version 03
# baseline (speedup 1.0000x reference)
import numpy as np
import ml_dtypes
import concourse.bass as bass
import concourse.mybir as mybir

F32 = mybir.dt.float32
BF16 = mybir.dt.bfloat16
Act = mybir.ActivationFunctionType
Alu = mybir.AluOpType
AX = mybir.AxisListType.X
bfdt = ml_dtypes.bfloat16

T, C, H, DK = 2048, 1024, 16, 64
NCH, NIC = 16, 8
FFN_H = 4096
C32 = 1.5 * 2.0 ** 23
C52 = 1.5 * 2.0 ** 52
LNSM = float(np.log(np.sqrt(128.0)))
DKS = float(DK ** -0.25)

ENGS = ("sync", "tensor", "vector", "scalar", "gpsimd")


def host_constants():
    ii = np.arange(128, dtype=np.float64)
    cdt = ii[None, :] - 1.0 - ii[:, None]
    cdt = np.where(cdt >= 0, cdt, 1e30).astype(np.float32)
    cc = np.arange(NCH, dtype=np.float64)
    cd2 = (cc[None, :] - 1.0 - cc[:, None]) * 128.0
    cd2 = np.where(cd2 >= 0, cd2, 1e30).astype(np.float32)
    return {
        "c_ident": np.eye(128).astype(bfdt),
        "c_cdt": cdt,
        "c_cd2": cd2,
        "c_iota_row": np.broadcast_to(np.arange(128, dtype=np.float32), (16, 128)).copy(),
        "c_rev127": np.broadcast_to((127.0 - np.arange(128, dtype=np.float32))[:, None], (128, 16)).copy(),
        "c_mask_ec": np.tril(np.ones((128, 128), np.float32)).T.copy(),
        "c_ones_row": np.ones((1, 128), dtype=bfdt),
        "c_ones_col": np.ones((128, 1), dtype=np.float32),
        "c_onesq_b": np.ones((128, 128), dtype=bfdt),
        "c_onesq_f": np.ones((128, 128), dtype=np.float32),
    }


def host_inputs(inp):
    A = np.ascontiguousarray
    f = np.float32
    d = {}
    d["wk_t"] = A(inp["L0_attn_a_Wk"].T.astype(f))
    d["wv_t"] = A(inp["L0_attn_a_Wv"].T.astype(f))
    d["wr_t"] = A(inp["L0_attn_a_Wr"].T.astype(f))
    d["wao_t"] = A(inp["L0_attn_a_Wo"].T.astype(f))
    d["mixk_d"] = A(inp["L0_attn_a_mix_k"].reshape(NIC, 128).T.astype(f))
    d["mixv_d"] = A(inp["L0_attn_a_mix_v"].reshape(NIC, 128).T.astype(f))
    d["mixr_d"] = A(inp["L0_attn_a_mix_r"].reshape(NIC, 128).T.astype(f))
    d["td_rep"] = np.broadcast_to(inp["L0_attn_a_time_decay"].astype(f), (128, 16)).copy()
    d["tf_rep"] = np.broadcast_to(inp["L0_attn_a_time_first"].astype(f), (128, 16)).copy()
    d["td_row16"] = A(inp["L0_attn_a_time_decay"].astype(f)[:, None])
    d["wq_t"] = A(inp["L0_attn_b_Wq"].T.astype(f))
    d["wkb_t"] = A(inp["L0_attn_b_Wk"].T.astype(f))
    d["wvb_t"] = A(inp["L0_attn_b_Wv"].T.astype(f))
    d["wbo_t"] = A(inp["L0_attn_b_Wo"].T.astype(f))
    d["bq_row"] = A(inp["L0_attn_b_bq"].astype(f)[None, :])
    d["bk_row"] = A(inp["L0_attn_b_bk"].astype(f)[None, :])
    d["bv_row"] = A(inp["L0_attn_b_bv"].astype(f)[None, :])
    d["bo_row"] = A(inp["L0_attn_b_bo"].astype(f)[None, :])
    d["proj_t"] = A(inp["L0_attn_b_proj"].T.astype(f))
    d["w1_t"] = A(inp["L0_ffn_w1"].T.astype(f))
    d["w2_t"] = A(inp["L0_ffn_w2"].T.astype(f))
    d["b1_row"] = A(inp["L0_ffn_b1"].astype(f)[None, :])
    d["b2_row"] = A(inp["L0_ffn_b2"].astype(f)[None, :])
    d.update(host_constants())
    return d


class KB:
    def __init__(self, nc, stack, debug_taps=(), stop_after=None):
        self.nc = nc
        self.stack = stack
        self.q = {e: [] for e in ENGS}
        self.semh = {}
        self.semv = {}
        self.taps = set(debug_taps)
        self.stop_after = stop_after
        self.dram = {}
        self.sb = {}
        self.ps = {}
        self.bankfree = {}
        self.hz = {}
        self.n_tap = 0

    def sem(self, name):
        if name not in self.semh:
            self.semh[name] = self.stack.enter_context(self.nc.semaphore(name))
            self.semv[name] = 0
        return self.semh[name]

    def emit(self, eng, fn, wait=(), inc=None, n=1):
        w = tuple((s, v) for (s, v) in wait if v > 0)
        for s, _ in w:
            self.sem(s)
        if w:
            def fw(e, w=w):
                for s, v in w:
                    e.wait_ge(self.semh[s], v)
            self.q[eng].append(fw)
        if inc is not None:
            self.sem(inc)
            self.semv[inc] += n
            def fi(e, fn=fn, inc=inc, n=n):
                fn(e).then_inc(self.semh[inc], n)
            self.q[eng].append(fi)
        else:
            self.q[eng].append(fn)

    def now(self, sem):
        self.sem(sem)
        return self.semv[sem]

    def sbuf(self, name, shape, dtype):
        t = self.stack.enter_context(self.nc.sbuf_tensor(name, list(shape), dtype))
        self.sb[name] = t
        return t

    def psum(self, name, shape, dtype=F32):
        t = self.stack.enter_context(self.nc.psum_tensor(name, list(shape), dtype))
        self.ps[name] = t
        return t

    def din(self, name, shape, dtype):
        ap = self.nc.dram_tensor(name, list(shape), dtype, kind="ExternalInput").ap()
        self.dram[name] = ap
        return ap

    def dout(self, name, shape, dtype):
        ap = self.nc.dram_tensor(name, list(shape), dtype, kind="ExternalOutput").ap()
        self.dram[name] = ap
        return ap

    def free_wait(self, bank):
        return (self.bankfree.get(bank, ("ve", 0)),)

    def hzw(self, name):
        t = self.hz.get(name)
        return (t,) if t else ()

    def hzset(self, name, sem):
        self.hz[name] = (sem, self.now(sem))

    def set_free(self, bank, sem):
        self.bankfree[bank] = (sem, self.now(sem))

    def tap(self, name, src_ap, wait=()):
        if name not in self.taps:
            return
        shape = list(src_ap.shape)
        ap = self.dout("tap_" + name, shape, src_ap.dtype)
        self.emit("sync", lambda e, ap=ap, s=src_ap: e.dma_start(out=ap, in_=s),
                  wait=wait, inc="dT", n=16)

    def declare(self):
        self.x_in = self.din("x", [T, C], F32)
        self.y_out = self.dout("y", [T, C], F32)
        for n, s in [("wk_t", [C, C]), ("wv_t", [C, C]), ("wr_t", [C, C]), ("wao_t", [C, C]),
                     ("mixk_d", [128, 8]), ("mixv_d", [128, 8]), ("mixr_d", [128, 8]),
                     ("td_rep", [128, 16]), ("tf_rep", [128, 16]), ("td_row16", [16, 1]),
                     ("wq_t", [C, C]), ("wkb_t", [C, C]), ("wvb_t", [C, C]), ("wbo_t", [C, C]),
                     ("bq_row", [1, C]), ("bk_row", [1, C]), ("bv_row", [1, C]), ("bo_row", [1, C]),
                     ("proj_t", [64, 128]),
                     ("w1_t", [C, FFN_H]), ("w2_t", [FFN_H, C]),
                     ("b1_row", [1, FFN_H]), ("b2_row", [1, C]),
                     ("c_cdt", [128, 128]), ("c_cd2", [16, 16]), ("c_iota_row", [16, 128]),
                     ("c_rev127", [128, 16]), ("c_mask_ec", [128, 128]),
                     ("c_ones_col", [128, 1]), ("c_onesq_f", [128, 128])]:
            self.din(n, s, F32)
        self.din("c_ident", [128, 128], BF16)
        self.din("c_ones_row", [1, 128], BF16)
        self.din("c_onesq_b", [128, 128], BF16)
        self.dram["w1q_hbm"] = self.nc.dram_tensor("w1q_hbm", [C, FFN_H], BF16, kind="Internal").ap()
        self.dram["w2q_hbm"] = self.nc.dram_tensor("w2q_hbm", [FFN_H, C], BF16, kind="Internal").ap()

        sb = self.sbuf
        sb("res", [128, NCH * C], F32)
        sb("warena", [128, 32768], BF16)
        sb("ident", [128, 128], BF16)
        sb("mask_ec", [128, 128], F32)
        sb("cdt", [128, 128], F32)
        sb("onesrow", [1, 128], BF16)
        sb("onesq_b", [128, 128], BF16)
        sb("onesq_f", [128, 128], F32)
        sb("wbuf", [128, 2048], BF16)
        sb("fcol", [128, 48], F32)
        sb("icol", [128, 1], F32)
        sb("jcol", [128, 1], F32)
        sb("one_c", [128, 1], F32)
        sb("colscr", [128, 128], F32)
        sb("mcol", [128, NCH], F32)
        sb("rcol", [128, NCH], F32)
        sb("bcol", [128, NCH], F32)
        sb("statS", [128, NCH], F32)
        sb("statQ", [128, NCH], F32)
        sb("mixk", [128, 8], F32)
        sb("mixv", [128, 8], F32)
        sb("mixr", [128, 8], F32)
        sb("wscr", [128, 352], F32)
        sb("dtall", [128, 16 * 128], BF16)
        sb("dtall2", [128, 16 * 128], BF16)
        sb("d2all", [16, 16 * 16], BF16)
        sb("epow16", [16, 128], BF16)
        sb("wcar", [128, 16], BF16)
        sb("eucol", [128, 16], F32)
        sb("lwb", [128, 16], F32)
        sb("uallN", [16, C], BF16)
        sb("uallD", [16, 16], BF16)
        sb("sA", [128, 4096], F32)
        sb("sB", [128, 2048], F32)
        sb("sC", [128, 2048], F32)
        sb("arenaB", [128, 14368], BF16)
        aB = self.sb["arenaB"]
        self.sb["sD"] = self.sb["sC"][:, 0:2048]
        self.sb["junkb"] = self.sb["sB"][:, 0:1024]
        self.sb["xlnb"] = aB[:, 0:1024]
        self.sb["xkb"] = aB[:, 1024:2048]
        self.sb["xvb"] = aB[:, 2048:3072]
        self.sb["xrb"] = aB[:, 3072:4096]
        self.sb["pchunk"] = aB[:, 4096:5120]
        self.sb["sgr"] = aB[:, 5120:6144]
        self.sb["gatedT"] = aB[:, 6144:7168]
        self.sb["xlnTc"] = aB[:, 7168:8224]
        self.sb["uscr"] = aB[:, 8224:12320]
        self.sb["eprows2"] = aB[:, 12320:14368]
        self.sb["xqb"] = aB[:, 0:1024]
        self.sb["xqT"] = aB[:, 1024:2048]
        self.sb["hqb"] = aB[:, 2048:4096]
        self.sb["hqT0"] = aB[:, 4096:8192]
        self.sb["hqT1"] = aB[:, 8192:12288]
        self.sb["brow"] = aB[:, 12288:14336]
        self.sb["wq_bf"] = aB[:, 0:4096]
        self.sb["qbT_b"] = aB[:, 0:1024]
        self.sb["kbT_b"] = aB[:, 1024:2048]
        self.sb["qfT_b"] = aB[:, 0:2048]
        self.sb["qfb_b"] = aB[:, 2048:4096]
        self.sb["kfT_b"] = aB[:, 2048:4096]
        self.sb["kfb_b"] = aB[:, 4096:6144]
        self.sb["vc65_b"] = aB[:, 6144:7184]
        self.sb["atm_b"] = (aB[:, 7184:7312], aB[:, 7312:7440])
        self.sb["ob_b"] = aB[:, 7440:8464]
        self.sb["xlnb_b"] = aB[:, 8464:9488]
        self.sb["qb_b"] = aB[:, 8464:9488]
        self.sb["xnT_b"] = aB[:, 9488:10512]
        self.sb["kb_b"] = aB[:, 9488:10512]
        self.sb["obT_b"] = aB[:, 9488:10512]
        self.sb["projT2"] = aB[:, 10512:10768]
        self.sb["Sb16_b"] = aB[:, 11248:12288]
        self.sb["brow_b"] = aB[:, 12288:14336]
        sb("ekzb", [128, 16], BF16)
        for i in range(7):
            self.psum("pb%d" % i, [128, 512], F32)
        self.psum("pbf", [128, 1024], BF16)

    def phase_load(self):
        g = self.sb
        xv = self.dram["x"].rearrange("(n p) d -> p n d", p=128)
        self.emit("sync", lambda e: e.dma_start(
            out=g["res"][:].rearrange("p (n d) -> p n d", n=NCH), in_=xv), inc="dX", n=16)
        for sb_n, dr_n in [("ident", "c_ident"), ("mask_ec", "c_mask_ec"), ("cdt", "c_cdt"),
                           ("onesrow", "c_ones_row"), ("one_c", "c_ones_col"),
                           ("onesq_b", "c_onesq_b"), ("onesq_f", "c_onesq_f"),
                           ("mixk", "mixk_d"), ("mixv", "mixv_d"), ("mixr", "mixr_d")]:
            self.emit("sync", lambda e, a=g[sb_n], b=self.dram[dr_n]: e.dma_start(out=a[:], in_=b),
                      inc="dX", n=16)
        self.x_loaded = self.now("dX")
        self.emit("vector", lambda e: e.tensor_scalar(out=g["icol"][:], in0=g["one_c"][:],
                                                      scalar1=C52, scalar2=-C52, op0=Alu.add, op1=Alu.add),
                  wait=(("dX", self.x_loaded),), inc="ve")
        self.emit("vector", lambda e: e.tensor_scalar(out=g["jcol"][:], in0=g["icol"][:],
                                                      scalar1=-1.0, scalar2=1.0, op0=Alu.mult, op1=Alu.add),
                  wait=(("ve", self.now("ve")),), inc="ve")
        self.gadget_ready = (("ve", self.now("ve")),)

    def ln_stats(self, wait0=()):
        g = self.sb
        jb = (g["sB"][:, 0:1024], g["sB"][:, 1024:2048])
        for n in range(NCH):
            sl = g["res"][:, C * n:C * n + C]
            wq = (("ve", self.now("ve")),) if n >= 2 else ()
            self.emit("scalar", lambda e, s=sl, n=n: e.activation(
                jb[n % 2], s, Act.Square),
                wait=(wait0 if n == 0 else ()) + wq, inc="sc")
            self.emit("vector", lambda e, s=sl, n=n: e.tensor_reduce(
                out=g["statS"][:, n:n + 1], in_=s, axis=AX, op=Alu.add),
                wait=wait0 if n == 0 else ())
            self.emit("vector", lambda e, n=n: e.tensor_reduce(
                out=g["statQ"][:, n:n + 1], in_=jb[n % 2], axis=AX, op=Alu.add),
                wait=(("sc", self.now("sc")),), inc="ve")
        wv = (("sc", self.now("sc")), ("ve", self.now("ve")))
        cs = g["colscr"]
        self.emit("vector", lambda e: e.tensor_scalar_mul(out=g["mcol"][:, 0:NCH],
                                                          in0=g["statS"][:, 0:NCH], scalar1=1.0 / C),
                  wait=wv, inc="ve")
        self.emit("vector", lambda e: e.scalar_tensor_tensor(
            out=cs[:, 16:32], in0=g["mcol"][:, 0:NCH], scalar=-1.0,
            in1=g["mcol"][:, 0:NCH], op0=Alu.mult, op1=Alu.mult),
            wait=(("ve", self.now("ve")),), inc="ve")
        self.emit("vector", lambda e: e.scalar_tensor_tensor(
            out=cs[:, 0:16], in0=g["statQ"][:, 0:NCH], scalar=1.0 / C,
            in1=cs[:, 16:32], op0=Alu.mult, op1=Alu.add),
            wait=(("ve", self.now("ve")),), inc="ve")
        self.emit("vector", lambda e: e.tensor_scalar_add(out=cs[:, 0:16], in0=cs[:, 0:16],
                                                          scalar1=1e-5),
                  wait=(("ve", self.now("ve")),), inc="ve")
        self.emit("scalar", lambda e: e.activation(cs[:, 112:128], cs[:, 0:16], Act.Sqrt),
                  wait=(("ve", self.now("ve")),), inc="sc")
        self.emit("vector", lambda e: e.reciprocal(out=g["rcol"][:, 0:NCH], in_=cs[:, 112:128]),
                  wait=(("sc", self.now("sc")),), inc="ve")
        self.emit("vector", lambda e: e.scalar_tensor_tensor(
            out=g["bcol"][:, 0:NCH], in0=g["mcol"][:, 0:NCH], scalar=-1.0,
            in1=g["rcol"][:, 0:NCH], op0=Alu.mult, op1=Alu.mult),
            wait=(("ve", self.now("ve")),), inc="ve")
        self.stats_done = (("ve", self.now("ve")),)
        self.tap("statS", g["statS"][:], wait=self.stats_done)
        self.tap("statQ", g["statQ"][:], wait=self.stats_done)
        self.tap("rcol", g["rcol"][:], wait=self.stats_done)
        self.tap("bcol", g["bcol"][:], wait=self.stats_done)
        self.tap("mcol", g["mcol"][:], wait=self.stats_done)
        self.tap("vareps", g["colscr"][:, 0:16], wait=self.stats_done)
        self.tap("sqrtout", g["colscr"][:, 112:128], wait=self.stats_done)
        self.tap("m2col", g["colscr"][:, 16:32], wait=self.stats_done)

    def load_warena_quadrant(self, dram_name, wi, scale=None):
        g = self.sb
        src = self.dram[dram_name].rearrange("(a p) o -> p a o", p=128)
        for part in range(4):
            qn, sm = ("sync", "dW") if part % 2 == 0 else ("gpsimd", "dWg")
            buf = (g["sD"] if part % 2 == 0 else g["sB"])[:, 0:2048]
            tok = self.hz.get("wl_%d" % (part % 2))
            self.emit(qn, lambda e, src=src, part=part, buf=buf: e.dma_start(
                out=buf.rearrange("p (a o) -> p a o", o=C), in_=src[:, 2 * part:2 * part + 2, :]),
                wait=(("ve", self.now("ve")),) if tok is None else (tok,), inc=sm, n=16)
            dst = g["warena"][:, 8192 * wi + 2048 * part: 8192 * wi + 2048 * part + 2048]
            if scale is None:
                self.emit("vector", lambda e, dst=dst, buf=buf: e.tensor_copy(out=dst, in_=buf),
                          wait=((sm, self.now(sm)),), inc="ve")
            else:
                self.emit("vector", lambda e, dst=dst, buf=buf, s=scale: e.tensor_scalar_mul(
                    out=dst, in0=buf, scalar1=s), wait=((sm, self.now(sm)),), inc="ve")
            self.hz["wl_%d" % (part % 2)] = ("ve", self.now("ve"))

    def attn_a_consts(self):
        g = self.sb
        cs = g["wscr"]
        self.emit("sync", lambda e: e.dma_start(out=cs[:, 0:16], in_=self.dram["td_rep"]), inc="dC", n=16)
        self.emit("sync", lambda e: e.dma_start(out=cs[:, 16:32], in_=self.dram["tf_rep"]), inc="dC", n=16)
        self.emit("sync", lambda e: e.dma_start(out=cs[:, 32:48], in_=self.dram["c_rev127"]), inc="dC", n=16)
        self.emit("sync", lambda e: e.dma_start(out=cs[0:16, 48:49], in_=self.dram["td_row16"]), inc="dC", n=16)
        self.emit("sync", lambda e: e.dma_start(out=cs[0:16, 64:80], in_=self.dram["c_cd2"]), inc="dC", n=16)
        self.emit("sync", lambda e: e.dma_start(out=cs[0:16, 80:208], in_=self.dram["c_iota_row"]), inc="dC", n=16)
        wtd = (("dC", self.now("dC")),)
        self.emit("scalar", lambda e: e.activation(g["lwb"][:, 0:16], cs[:, 0:16], Act.Exp), wait=wtd)
        self.emit("scalar", lambda e: e.activation(g["eucol"][:, 0:16], cs[:, 16:32], Act.Exp), inc="sc")
        self.emit("vector", lambda e: e.tensor_scalar_mul(out=g["lwb"][:, 0:16], in0=g["lwb"][:, 0:16],
                                                          scalar1=-1.0), wait=(("sc", self.now("sc")),), inc="ve")
        wlw = (("ve", self.now("ve")),)
        for h in range(H):
            self.emit("scalar", lambda e, h=h: e.activation(
                g["dtall"][:, 128 * h:128 * h + 128], g["cdt"][:], Act.Exp,
                scale=g["lwb"][:, h:h + 1]), wait=wlw if h == 0 else (), inc="sc")
        self.emit("vector", lambda e: e.tensor_tensor(out=cs[:, 336:352], in0=cs[:, 32:48],
                                                      in1=g["lwb"][:, 0:16], op=Alu.mult), wait=wlw, inc="ve")
        self.emit("scalar", lambda e: e.activation(g["wcar"][:, 0:16], cs[:, 336:352], Act.Exp),
                  wait=(("ve", self.now("ve")),), inc="sc")
        self.emit("scalar", lambda e: e.activation(cs[0:16, 49:50], cs[0:16, 48:49], Act.Exp), wait=wtd, inc="sc")
        self.emit("vector", lambda e: e.tensor_scalar_mul(out=cs[0:16, 50:51], in0=cs[0:16, 49:50],
                                                          scalar1=-1.0), wait=(("sc", self.now("sc")),), inc="ve")
        wlw16 = (("ve", self.now("ve")),)
        for h in range(H):
            self.emit("scalar", lambda e, h=h: e.activation(
                g["d2all"][0:16, 16 * h:16 * h + 16], cs[0:16, 64:80], Act.Exp,
                scale=g["lwb"][0:16, h:h + 1]), wait=wlw16 if h == 0 else (), inc="sc")
        self.emit("scalar", lambda e: e.activation(cs[0:16, 208:336], cs[0:16, 80:208], Act.Exp,
                                                   scale=cs[0:16, 50:51]), wait=wlw16, inc="sc")
        self.emit("vector", lambda e: e.tensor_copy(out=g["epow16"][:], in_=cs[0:16, 208:336]),
                  wait=(("sc", self.now("sc")),), inc="ve")
        wdt2 = (("sc", self.now("sc")),)
        for h in range(H):
            self.emit("vector", lambda e, h=h: e.tensor_scalar_mul(
                out=g["sA"][:, 128 * h:128 * h + 128], in0=g["ident"][:],
                scalar1=g["eucol"][:, h:h + 1]),
                wait=wdt2 if h == 0 else (), inc="ve" if h == H - 1 else None)
        wdt3 = (("ve", self.now("ve")),)
        for h in range(H):
            self.emit("vector", lambda e, h=h: e.tensor_tensor(
                out=g["dtall2"][:, 128 * h:128 * h + 128], in0=g["dtall"][:, 128 * h:128 * h + 128],
                in1=g["sA"][:, 128 * h:128 * h + 128], op=Alu.add),
                wait=wdt3 if h == 0 else (), inc="ve" if h == H - 1 else None)
        self.emit("sync", lambda e: e.dma_start(out=g["eprows2"][32:33, :], in_=g["epow16"][:]),
                  wait=(("ve", self.now("ve")),), inc="dC", n=16)
        self.emit("sync", lambda e: e.dma_start(out=g["eprows2"][64:65, :], in_=g["epow16"][:]),
                  inc="dC", n=16)
        self.aconsts_done = (("dC", self.now("dC")), ("sc", self.now("sc")))

    def phase_attn_a(self):
        g = self.sb
        self.attn_a_consts()
        self.hz.pop("wl_0", None)
        self.hz.pop("wl_1", None)
        for wi, nm in enumerate(["wk_t", "wv_t", "wr_t", "wao_t"]):
            self.load_warena_quadrant(nm, wi)
        self.wa_loaded = (("ve", self.now("ve")),)
        self.ln_stats(wait0=(("dX", self.x_loaded),) + (("ve", self.now("ve")),))
        self.emit("vector", lambda e: e.memset(g["xlnTc"][:], 0.0), inc="ve")
        self.emit("vector", lambda e: e.memset(g["uallN"][:], 0.0), inc="ve")
        self.emit("vector", lambda e: e.memset(g["uallD"][:], 0.0), inc="ve")
        self.hz["init_ms"] = ("ve", self.now("ve"))
        for c in range(getattr(self, "nchunks", NCH)):
            self.attn_a_chunk(c)
        self.tap("res_a", g["res"][:], wait=(("ve", self.now("ve")),))
        self.tap("uallN", g["uallN"][:], wait=(("dS", self.now("dS")),))
        self.tap("uallD", g["uallD"][:], wait=(("dS", self.now("dS")),))
        self.tap("eprows", g["eprows2"][0:1, :], wait=(("dS", self.now("dS")),))
        self.tap("dtall", g["dtall"][:], wait=(("sc", self.now("sc")),))
        self.tap("d2all", g["d2all"][:], wait=(("sc", self.now("sc")),))
        self.tap("wcar", g["wcar"][:], wait=(("sc", self.now("sc")),))

    def attn_a_chunk(self, c):
        g = self.sb
        p = self.ps
        W = g["warena"]
        ms = getattr(self, "maxstep", 99)
        resc = g["res"][:, C * c:C * c + C]
        w0 = self.stats_done + self.wa_loaded + self.aconsts_done + self.hzw("wq_dW") if c == 0 else ()
        self.emit("scalar", lambda e: e.activation(
            g["xlnb"][:], resc, Act.Identity, bias=g["bcol"][:, c:c + 1],
            scale=g["rcol"][:, c:c + 1]), wait=w0 + self.hzw("xlnT_tn"), inc="sc")
        wtp = (("sc", self.now("sc")),) + self.free_wait("pbf")
        for i in range(NIC):
            self.emit("tensor", lambda e, i=i: e.transpose(
                p["pbf"][:, 128 * i:128 * i + 128], g["xlnb"][:, 128 * i:128 * i + 128],
                g["ident"][:]), wait=wtp if i == 0 else (), inc="tn")
        self.hzset("xlnT_tn", "tn")
        wt = (("tn", self.now("tn")),)
        x3 = g["xlnTc"][:].rearrange("p (a q) -> p a q", q=132)
        self.emit("vector", lambda e: e.tensor_copy(out=x3[:, :, 0:1], in_=x3[:, :, 128:129]),
                  wait=self.hzw("mix_gp") + self.hzw("xevac_ve") + (self.hzw("init_ms") if c == 0 else ()),
                  inc="ve")
        self.emit("vector", lambda e: e.tensor_copy(
            out=x3[:, :, 1:129], in_=p["pbf"][:].rearrange("p (a q) -> p a q", q=128)),
            wait=wt + (("ve", self.now("ve")),), inc="ve")
        self.set_free("pbf", "ve")
        self.hzset("xevac_ve", "ve")
        wx = (("ve", self.now("ve")),)
        xsh = x3[:, :, 0:128]
        xcur = x3[:, :, 1:129]
        Dm = g["sA"][:, 0:1024].rearrange("p (a q) -> p a q", q=128)
        tmp = g["sA"][:, 1024:2048].rearrange("p (a q) -> p a q", q=128)
        tmp2 = g["sB"][:, 0:1024].rearrange("p (a q) -> p a q", q=128)
        tmp3 = g["sB"][:, 1024:2048].rearrange("p (a q) -> p a q", q=128)
        self.emit("vector", lambda e: e.tensor_sub(out=Dm, in0=xcur, in1=xsh), wait=wx, inc="ve")
        wD = (("ve", self.now("ve")),)
        if ms < 2:
            return
        wreuse = self.hzw("proj_tn") + self.hzw("gated_ve")
        def mixop(eng, mixname, dst):
            w = wreuse + wD
            for ic in range(NIC):
                self.emit(eng, lambda e, ic=ic, mixname=mixname, dst=dst: e.scalar_tensor_tensor(
                    out=g[dst][:, 128 * ic:128 * ic + 128], in0=Dm[:, ic, :],
                    scalar=g[mixname][:, ic:ic + 1], in1=xsh[:, ic, :],
                    op0=Alu.mult, op1=Alu.add), wait=w if ic == 0 else (),
                    inc={"vector": "ve", "gpsimd": "gp"}[eng] if ic == NIC - 1 else None)
        mixop("vector", "mixk", "xkb")
        mixop("vector", "mixv", "xvb")
        mixop("vector", "mixr", "xrb")
        self.hzset("mix_gp", "ve")
        if ms < 3:
            return
        wmix = (("ve", self.now("ve")),)
        for wi, src, bk in ((0, "xkb", ("pb0", "pb1")), (1, "xvb", ("pb2", "pb3")),
                            (2, "xrb", ("pb4", "pb5"))):
            wfree = self.free_wait(bk[0]) + self.free_wait(bk[1])
            for ic in range(NIC):
                for oh in range(2):
                    self.emit("tensor", lambda e, wi=wi, src=src, bk=bk, ic=ic, oh=oh: e.matmul(
                        p[bk[oh]][:], g[src][:, 128 * ic:128 * ic + 128],
                        W[:, 8192 * wi + 1024 * ic + 512 * oh: 8192 * wi + 1024 * ic + 512 * oh + 512],
                        start=(ic == 0), stop=(ic == NIC - 1)),
                        wait=(wmix + wfree) if (ic == 0 and oh == 0) else (),
                        inc="tn" if (ic == NIC - 1 and oh == 1) else None)
        self.hzset("proj_tn", "tn")
        if ms < 4:
            return
        wproj = (("tn", self.now("tn")),)
        ek = g["sA"][:, 0:1024]
        self.emit("scalar", lambda e: e.activation(ek[:, 0:512], p["pb0"][:], Act.Exp), wait=wproj)
        self.emit("scalar", lambda e: e.activation(ek[:, 512:1024], p["pb1"][:], Act.Exp), inc="sc")
        self.set_free("pb0", "sc")
        self.set_free("pb1", "sc")
        SGF = getattr(self, "sgf", Act.Sigmoid)
        self.emit("scalar", lambda e: e.activation(g["sgr"][:, 0:512], p["pb4"][:], SGF),
                  wait=self.hzw("gated_ve"))
        self.emit("scalar", lambda e: e.activation(g["sgr"][:, 512:1024], p["pb5"][:], SGF), inc="sc")
        self.set_free("pb4", "sc")
        self.set_free("pb5", "sc")
        wek = (("sc", self.now("sc")),)
        self.emit("vector", lambda e: e.tensor_mul(out=g["pchunk"][:, 0:512], in0=ek[:, 0:512],
                                                   in1=p["pb2"][:]), wait=wek + self.hzw("gtr_tn"))
        self.emit("vector", lambda e: e.tensor_mul(out=g["pchunk"][:, 512:1024], in0=ek[:, 512:1024],
                                                   in1=p["pb3"][:]), inc="ve")
        self.set_free("pb2", "ve")
        self.set_free("pb3", "ve")
        self.emit("vector", lambda e: e.tensor_reduce(
            out=g["colscr"][:, 96:112], in_=ek.rearrange("p (h d) -> p h d", d=64), axis=AX, op=Alu.add),
            wait=self.hzw("dp_tn"), inc="ve")
        self.emit("vector", lambda e: e.tensor_copy(out=g["ekzb"][:], in_=g["colscr"][:, 96:112]),
                  wait=(("ve", self.now("ve")),), inc="ve")
        wP = (("ve", self.now("ve")),)
        if c == 0:
            self.tap("xlnTc_c0", g["xlnTc"][:], wait=(("ve", self.now("ve")),))
            self.tap("xkb_c0", g["xkb"][:], wait=(("ve", self.now("ve")),))
            self.tap("pch_c0", g["pchunk"][:], wait=wP)
            self.tap("ekzb_c0", g["ekzb"][:], wait=wP)
            self.tap("sgr_c0", g["sgr"][:], wait=(("sc", self.now("sc")),))
            self.tap("ek_c0", g["sA"][:, 0:1024], wait=(("sc", self.now("sc")),))
        if ms < 5:
            return
        noState = getattr(self, "noState", False)
        s5 = getattr(self, "s5", 9)
        wu = wP + self.free_wait("pb4") + self.free_wait("pb5")
        mms = []
        for h in range(H if not noState else 0):
            bku = "pb4" if h < 8 else "pb5"
            off = 64 * (h % 8)
            mms.append((lambda e, h=h, bku=bku, off=off: e.matmul(
                p[bku][0:1, off:off + 64], g["wcar"][:, h:h + 1], g["pchunk"][:, 64 * h:64 * h + 64],
                start=True, stop=True), wu if h == 0 else ()))
            mms.append((lambda e, h=h: e.matmul(
                p["pb4"][32:33, h:h + 1], g["wcar"][:, h:h + 1], g["ekzb"][:, h:h + 1],
                start=True, stop=True), ()))
        s = c % 2
        if c > 0 and not noState and s5 >= 2:
            wsn = (("dS", self.uscat_prev),) + (self.hzw("init_ms") if c == 1 else ())
            for h in range(H):
                bks = "pb4" if h < 8 else "pb5"
                off = 64 * (h % 8)
                mms.append((lambda e, h=h, bks=bks, off=off, c=c: e.matmul(
                    p[bks][64:65, off:off + 64], g["d2all"][0:16, 16 * h + c:16 * h + c + 1],
                    g["uallN"][0:16, 64 * h:64 * h + 64], start=True, stop=True),
                    wsn if h == 0 else ()))
                mms.append((lambda e, h=h, c=c: e.matmul(
                    p["pb4"][32:33, 16 + h:16 + h + 1], g["d2all"][0:16, 16 * h + c:16 * h + c + 1],
                    g["uallD"][0:16, h:h + 1], start=True, stop=True), ()))
        if mms:
            for i, (f, w) in enumerate(mms):
                self.emit("tensor", f, wait=w, inc="tn" if i == len(mms) - 1 else None)
        else:
            self.emit("tensor", lambda e: e.nop(), inc="tn")
        wUm = (("tn", self.now("tn")),)
        if not noState and s5 >= 3:
            self.emit("scalar", lambda e, s=s: e.activation(
                g["uscr"][:, 2048 * s:2048 * s + 512], p["pb4"][:], Act.Identity,
                bias=0.0, scale=1.0),
                wait=wUm + self.hzw("uscat_dS") + self.hzw("uscr%d_tn" % s))
            self.emit("scalar", lambda e, s=s: e.activation(
                g["uscr"][:, 2048 * s + 512:2048 * s + 1024], p["pb5"][:], Act.Identity,
                bias=0.0, scale=1.0), inc="sc")
        elif not noState and s5 < 3:
            self.emit("scalar", lambda e: e.nop(), wait=wUm, inc="sc")
        else:
            self.emit("vector", lambda e, s=s: e.memset(g["uscr"][:, 2048 * s:2048 * s + 2048], 0.0),
                      wait=wUm, inc="sc")
        wur = (("sc", self.now("sc")),)
        self.set_free("pb4", "sc")
        self.set_free("pb5", "sc")
        if not noState and s5 >= 4:
            self.emit("sync", lambda e, s=s: e.dma_start(out=g["uallN"][c:c + 1, :],
                                                         in_=g["uscr"][0:1, 2048 * s:2048 * s + 1024]),
                      wait=wur + (("dS", self.now("dS")),) + (self.hzw("init_ms") if c == 0 else ()),
                      inc="dS", n=16)
            self.emit("sync", lambda e, s=s: e.dma_start(out=g["uallD"][c:c + 1, :],
                                                         in_=g["uscr"][32:33, 2048 * s:2048 * s + 16]),
                      inc="dS", n=16)
            self.hzset("uscat_dS", "dS")
            self.uscat = self.now("dS")
        self.uscat_prev = getattr(self, "uscat", 0)
        wsr = (("sc", self.now("sc")),)
        if ms < 7:
            return
        wdp = wsr + wP + self.free_wait("pb2") + self.free_wait("pb3") + self.free_wait("pb6")
        mms = []
        for h in range(H):
            bkn = "pb2" if h < 8 else "pb3"
            off = 64 * (h % 8)
            noR1 = getattr(self, "noR1", False) or c == 0
            mms.append((lambda e, h=h, bkn=bkn, off=off, noR1=noR1: e.matmul(
                p[bkn][:, off:off + 64], g["dtall2"][:, 128 * h:128 * h + 128],
                g["pchunk"][:, 64 * h:64 * h + 64], start=True, stop=noR1),
                wdp if h == 0 else ()))
            if not noR1:
                mms.append((lambda e, h=h, bkn=bkn, off=off, s=s: e.matmul(
                    p[bkn][:, off:off + 64], g["eprows2"][64:65, 128 * h:128 * h + 128],
                    g["uscr"][64:65, 2048 * s + 512 * (h // 8) + 64 * (h % 8):
                              2048 * s + 512 * (h // 8) + 64 * (h % 8) + 64], start=False, stop=True), ()))
            mms.append((lambda e, h=h, s=s, noR1=noR1: e.matmul(
                p["pb6"][:, 32 * s + h:32 * s + h + 1], g["dtall"][:, 128 * h:128 * h + 128],
                g["ekzb"][:, h:h + 1], start=True, stop=noR1), ()))
            if not noR1:
                mms.append((lambda e, h=h, s=s: e.matmul(
                    p["pb6"][:, 32 * s + h:32 * s + h + 1], g["eprows2"][32:33, 128 * h:128 * h + 128],
                    g["uscr"][32:33, 2048 * s + 16 + h:2048 * s + 16 + h + 1], start=False, stop=True), ()))
        for i, (f, w) in enumerate(mms):
            self.emit("tensor", f, wait=w, inc="tn" if i == len(mms) - 1 else None)
        self.hzset("dp_tn", "tn")
        self.hzset("uscr%d_tn" % s, "tn")
        wdpd = (("tn", self.now("tn")),)
        if ms < 8:
            return
        cs = g["colscr"]
        self.emit("vector", lambda e: e.tensor_mul(out=cs[:, 32:48], in0=g["eucol"][:, 0:16],
                                                   in1=g["ekzb"][:, 0:16]), wait=wdpd, inc="ve")
        self.emit("vector", lambda e: e.tensor_scalar_max(out=cs[:, 32:48], in0=cs[:, 32:48], scalar1=1e-6),
                  wait=(("ve", self.now("ve")),), inc="ve")
        self.emit("vector", lambda e, s=s: e.tensor_tensor(out=cs[:, 48:64], in0=cs[:, 32:48],
                                                           in1=p["pb6"][:, 32 * s:32 * s + 16], op=Alu.add),
                  wait=(("ve", self.now("ve")),), inc="ve")
        self.set_free("pb6", "ve")
        self.emit("vector", lambda e: e.reciprocal(out=cs[:, 64:80], in_=cs[:, 48:64]),
                  wait=(("ve", self.now("ve")),), inc="ve")
        wrec2 = (("ve", self.now("ve")),)
        for h in range(H):
            bkn = "pb2" if h < 8 else "pb3"
            off = 64 * (h % 8)
            self.emit("vector", lambda e, h=h, bkn=bkn, off=off: e.scalar_tensor_tensor(
                out=g["pchunk"][:, 64 * h:64 * h + 64], in0=p[bkn][:, off:off + 64],
                scalar=cs[:, 64 + h:64 + h + 1], in1=g["sgr"][:, 64 * h:64 * h + 64],
                op0=Alu.mult, op1=Alu.mult), wait=wrec2 if h == 0 else (),
                inc="ve" if h == H - 1 else None)
        self.set_free("pb2", "ve")
        self.set_free("pb3", "ve")
        self.hzset("gated_ve", "ve")
        wg = (("ve", self.now("ve")),) + self.free_wait("pbf")
        if ms < 9:
            return
        for i in range(NIC):
            self.emit("tensor", lambda e, i=i: e.transpose(
                p["pbf"][:, 128 * i:128 * i + 128], g["pchunk"][:, 128 * i:128 * i + 128],
                g["ident"][:]), wait=wg if i == 0 else (), inc="tn")
        self.hzset("gtr_tn", "tn")
        self.emit("vector", lambda e: e.tensor_copy(out=g["gatedT"][:], in_=p["pbf"][:]),
                  wait=(("tn", self.now("tn")),) + self.hzw("wo_tn"), inc="ve")
        self.set_free("pbf", "ve")
        wgt = (("ve", self.now("ve")),) + self.free_wait("pb0") + self.free_wait("pb1")
        if ms < 10:
            return
        for ic in range(NIC):
            for oh in range(2):
                self.emit("tensor", lambda e, ic=ic, oh=oh: e.matmul(
                    p["pb0" if oh == 0 else "pb1"][:], g["gatedT"][:, 128 * ic:128 * ic + 128],
                    W[:, 8192 * 3 + 1024 * ic + 512 * oh: 8192 * 3 + 1024 * ic + 512 * oh + 512],
                    start=(ic == 0), stop=(ic == NIC - 1)),
                    wait=wgt if (ic == 0 and oh == 0) else (),
                    inc="tn" if (ic == NIC - 1 and oh == 1) else None)
        self.hzset("wo_tn", "tn")
        wwo = (("tn", self.now("tn")),)
        self.emit("vector", lambda e: e.tensor_add(out=resc[:, 0:512], in0=resc[:, 0:512],
                                                   in1=p["pb0"][:]), wait=wwo)
        self.emit("vector", lambda e: e.tensor_add(out=resc[:, 512:1024], in0=resc[:, 512:1024],
                                                   in1=p["pb1"][:]), inc="ve")
        self.set_free("pb0", "ve")
        self.set_free("pb1", "ve")

    def vchain(self, ops, wait0=()):
        for i, f in enumerate(ops):
            w = wait0 if i == 0 else (("ve", self.now("ve")),)
            self.emit("vector", f, wait=w, inc="ve")

    def wquant_one(self, dram_in, dram_out, ncols, nch, iswcol):
        g = self.sb
        p = self.ps
        src = self.dram[dram_in].rearrange("(a p) o -> p a o", p=128)
        dst = self.dram[dram_out].rearrange("(a p) o -> p a o", p=128)
        stage = g["sA"][:, 0:ncols]
        nh = (ncols + 2047) // 2048
        for a in range(nch):
            self.emit("sync", lambda e, a=a: e.dma_start(out=stage, in_=src[:, a, :]),
                      wait=(("ve", self.now("ve")),), inc="dW", n=16)
            self.emit("vector", lambda e, a=a: e.tensor_reduce(
                out=g["fcol"][:, a:a + 1], in_=stage, axis=AX, op=Alu.add,
                apply_absolute_value=True), wait=(("dW", self.now("dW")),), inc="ve")
        self.vchain([
            lambda e: e.tensor_reduce(out=g["fcol"][:, 32:33], in_=g["fcol"][:, 0:nch],
                                      axis=AX, op=Alu.add)],
            wait0=(("ve", self.now("ve")),))
        self.emit("tensor", lambda e: e.matmul(p["pb6"][:, 0:1], g["onesq_f"][:],
                                               g["fcol"][:, 32:33], start=True, stop=True),
                  wait=(("ve", self.now("ve")),) + self.free_wait("pb6"), inc="tn")
        self.emit("scalar", lambda e: e.activation(g["fcol"][:, 33:34], p["pb6"][:, 0:1],
                                                   Act.Identity, bias=0.0, scale=1.0),
                  wait=(("tn", self.now("tn")),), inc="sc")
        self.set_free("pb6", "sc")
        nel = float(nch * 128 * ncols)
        self.vchain([
            lambda e: e.tensor_scalar(out=iswcol, in0=g["fcol"][:, 33:34],
                                      scalar1=1.0 / nel, scalar2=1e-5,
                                      op0=Alu.mult, op1=Alu.max),
            lambda e: e.reciprocal(out=g["fcol"][:, 35:36], in_=iswcol)],
            wait0=(("sc", self.now("sc")),))
        wsw = (("ve", self.now("ve")),)
        for a in range(nch):
            self.emit("sync", lambda e, a=a: e.dma_start(out=stage, in_=src[:, a, :]),
                      wait=(("ve", self.now("ve")),), inc="dW", n=16)
            wd = (("dW", self.now("dW")),) + (wsw if a == 0 else ())
            for hh in range(nh):
                w = min(2048, ncols - 2048 * hh)
                sl = slice(2048 * hh, 2048 * hh + w)
                self.emit("scalar", lambda e, sl=sl, w=w: e.activation(
                    g["sB"][:, 0:w], stage[:, sl], Act.Identity, bias=0.0,
                    scale=g["fcol"][:, 35:36]),
                    wait=wd + (("ve", self.now("ve")),), inc="sc")
                self.vchain([
                    lambda e, w=w: e.tensor_scalar(out=g["sC"][:, 0:w], in0=g["sB"][:, 0:w],
                                                   scalar1=C32, scalar2=-C32, op0=Alu.add, op1=Alu.add),
                    lambda e, w=w: e.tensor_scalar(out=g["sB"][:, 0:w], in0=g["sB"][:, 0:w],
                                                   scalar1=C52, scalar2=-C52, op0=Alu.add, op1=Alu.add),
                    lambda e, w=w: e.tensor_scalar_mul(out=g["sB"][:, 0:w], in0=g["sB"][:, 0:w],
                                                       scalar1=g["icol"][:]),
                    lambda e, w=w: e.scalar_tensor_tensor(out=g["sC"][:, 0:w], in0=g["sC"][:, 0:w],
                                                          scalar=g["jcol"][:], in1=g["sB"][:, 0:w],
                                                          op0=Alu.mult, op1=Alu.add),
                ], wait0=(("sc", self.now("sc")),))
                self.emit("vector", lambda e, sl=sl, w=w: e.tensor_scalar(
                    out=g["wq_bf"][:, sl], in0=g["sC"][:, 0:w], scalar1=-1.0, scalar2=1.0,
                    op0=Alu.max, op1=Alu.min),
                    wait=(("ve", self.now("ve")),) +
                         ((("dWs", self.now("dWs")),) if hh == 0 else ()), inc="ve")
            self.emit("sync", lambda e, a=a: e.dma_start(out=dst[:, a, :], in_=g["wq_bf"][:, 0:ncols]),
                      wait=(("ve", self.now("ve")),), inc="dWs", n=16)
        self.hz["wq_dW"] = ("dWs", self.now("dWs"))

    def phase_wquant(self):
        g = self.sb
        self.wquant_one("w1_t", "w1q_hbm", FFN_H, NIC, g["fcol"][:, 40:41])
        self.wquant_one("w2_t", "w2q_hbm", C, 32, g["fcol"][:, 41:42])

    def phase_ffn(self, idx):
        g = self.sb
        p = self.ps
        W = g["warena"]
        wprev = (("tn", self.now("tn")), ("sc", self.now("sc")), ("ve", self.now("ve")))
        w1src = self.dram["w1q_hbm"].rearrange("(a p) o -> p a o", p=128)
        self.emit("sync", lambda e: e.dma_start(
            out=W[:].rearrange("p (a o) -> p a o", o=FFN_H), in_=w1src),
            wait=wprev + self.hzw("wq_dW"), inc="dW", n=16)
        self.emit("sync", lambda e: e.dma_start(out=g["sD"][0:1, 0:2048],
                                                in_=self.dram["b1_row"][0:1, 0:2048]), inc="dW", n=16)
        self.emit("sync", lambda e: e.dma_start(out=g["sD"][32:33, 0:2048],
                                                in_=self.dram["b1_row"][0:1, 2048:4096]), inc="dW", n=16)
        self.emit("sync", lambda e: e.dma_start(out=g["sD"][64:65, 0:1024],
                                                in_=self.dram["b2_row"]), inc="dW", n=16)
        wbias = (("dW", self.now("dW")),)
        self.emit("vector", lambda e: e.tensor_copy(out=g["brow"][0:1, :], in_=g["sD"][0:1, 0:2048]),
                  wait=wbias + wprev)
        self.emit("vector", lambda e: e.tensor_copy(out=g["brow"][32:33, :], in_=g["sD"][32:33, 0:2048]))
        self.emit("vector", lambda e: e.tensor_copy(out=g["brow"][64:65, 0:1024],
                                                    in_=g["sD"][64:65, 0:1024]), inc="ve")
        self.wffn = (("dW", self.now("dW")), ("ve", self.now("ve")))
        self.ln_stats(wait0=(("ve", self.now("ve")),))
        self.hz["ffn_dS"] = ("dS", self.now("dS"))
        for n2 in range(0, NCH, 2):
            self.ffn_pair(n2)
        self.tap("res_f%d" % idx, g["res"][:], wait=(("ve", self.now("ve")),))

    def ffn_quant_cols(self, n):
        g = self.sb
        fc = g["fcol"]
        resc = g["res"][:, C * n:C * n + C]
        self.emit("vector", lambda e, n=n: e.reduce_max(out=fc[:, 0:1], in_=resc, axis=AX),
                  wait=self.stats_done if n == 0 else (), inc="ve")
        self.emit("vector", lambda e, n=n: e.tensor_reduce(out=fc[:, 1:2], in_=resc, axis=AX,
                                                           op=Alu.min),
                  wait=(("ve", self.now("ve")),), inc="ve")
        self.vchain([
            lambda e, n=n: e.reciprocal(out=fc[:, 2:3], in_=g["rcol"][:, n:n + 1]),
            lambda e: e.tensor_tensor(out=fc[:, 2:3], in0=fc[:, 2:3], in1=fc[:, 2:3], op=Alu.mult),
            lambda e: e.tensor_scalar(out=fc[:, 2:3], in0=fc[:, 2:3], scalar1=-1e-5,
                                      scalar2=float(1 + 1e-6), op0=Alu.add, op1=Alu.mult),
            lambda e: e.tensor_scalar_add(out=fc[:, 2:3], in0=fc[:, 2:3], scalar1=1e-11),
        ], wait0=(("ve", self.now("ve")),))
        self.emit("scalar", lambda e: e.activation(fc[:, 3:4], fc[:, 2:3], Act.Sqrt),
                  wait=(("ve", self.now("ve")),), inc="sc")
        self.vchain([
            lambda e: e.reciprocal(out=fc[:, 4:5], in_=fc[:, 3:4]),
            lambda e, n=n: e.tensor_tensor(out=fc[:, 5:6], in0=fc[:, 0:1],
                                           in1=g["mcol"][:, n:n + 1], op=Alu.subtract),
            lambda e, n=n: e.tensor_tensor(out=fc[:, 6:7], in0=g["mcol"][:, n:n + 1],
                                           in1=fc[:, 1:2], op=Alu.subtract),
            lambda e: e.tensor_tensor(out=fc[:, 5:6], in0=fc[:, 5:6], in1=fc[:, 6:7], op=Alu.max),
            lambda e: e.tensor_tensor(out=fc[:, 5:6], in0=fc[:, 5:6], in1=fc[:, 4:5], op=Alu.mult),
            lambda e: e.tensor_scalar_max(out=fc[:, 5:6], in0=fc[:, 5:6], scalar1=1e-5),
            lambda e: e.reciprocal(out=fc[:, 6:7], in_=fc[:, 5:6]),
            lambda e: e.tensor_scalar_mul(out=fc[:, 6:7], in0=fc[:, 6:7], scalar1=127.0),
            lambda e: e.tensor_tensor(out=fc[:, 8:9], in0=fc[:, 4:5], in1=fc[:, 6:7], op=Alu.mult),
            lambda e, n=n: e.scalar_tensor_tensor(out=fc[:, 9:10], in0=g["mcol"][:, n:n + 1],
                                                  scalar=-1.0, in1=fc[:, 8:9],
                                                  op0=Alu.mult, op1=Alu.mult),
            lambda e: e.tensor_tensor(out=fc[:, 7:8], in0=fc[:, 5:6], in1=g["fcol"][:, 40:41],
                                      op=Alu.mult),
            lambda e: e.tensor_scalar_mul(out=fc[:, 7:8], in0=fc[:, 7:8], scalar1=1.0 / 127.0),
            lambda e: e.tensor_tensor(out=fc[:, 10:11], in0=fc[:, 7:8], in1=g["jcol"][:], op=Alu.mult),
            lambda e: e.tensor_tensor(out=fc[:, 11:12], in0=fc[:, 7:8], in1=g["icol"][:], op=Alu.mult),
        ], wait0=(("sc", self.now("sc")),))

    def ffn_tile_h(self, n, slot):
        g = self.sb
        p = self.ps
        fc = g["fcol"]
        W = g["warena"]
        resc = g["res"][:, C * n:C * n + C]
        self.ffn_quant_cols(n)
        wcols = (("ve", self.now("ve")),)
        self.emit("scalar", lambda e: e.activation(g["sC"][:, 0:1024], resc, Act.Identity,
                                                   bias=fc[:, 9:10], scale=fc[:, 8:9]),
                  wait=wcols, inc="sc")
        self.vchain([
            lambda e: e.tensor_scalar(out=g["sC"][:, 1024:2048], in0=g["sC"][:, 0:1024],
                                      scalar1=C32, scalar2=-C32, op0=Alu.add, op1=Alu.add),
            lambda e: e.tensor_scalar(out=g["sB"][:, 0:1024], in0=g["sC"][:, 0:1024],
                                      scalar1=C52, scalar2=-C52, op0=Alu.add, op1=Alu.add),
            lambda e: e.tensor_scalar_mul(out=g["sB"][:, 0:1024], in0=g["sB"][:, 0:1024],
                                          scalar1=fc[:, 11:12]),
            lambda e: e.scalar_tensor_tensor(out=g["xqb"][:], in0=g["sC"][:, 1024:2048],
                                             scalar=fc[:, 10:11], in1=g["sB"][:, 0:1024],
                                             op0=Alu.mult, op1=Alu.add),
        ], wait0=(("sc", self.now("sc")),))
        wq = (("ve", self.now("ve")),) + self.free_wait("pbf")
        for i in range(NIC):
            self.emit("tensor", lambda e, i=i: e.transpose(
                p["pbf"][:, 128 * i:128 * i + 128], g["xqb"][:, 128 * i:128 * i + 128],
                g["ident"][:]), wait=wq if i == 0 else (), inc="tn" if i == NIC - 1 else None)
        self.emit("vector", lambda e: e.tensor_copy(out=g["xqT"][:], in_=p["pbf"][:]),
                  wait=(("tn", self.now("tn")),) + self.hzw("xqT_tn"), inc="ve")
        self.set_free("pbf", "ve")
        wxt = (("ve", self.now("ve")),)
        for ho in range(2):
            wfree = tuple(self.free_wait("pb%d" % b)[0] for b in range(4))
            mms = []
            for ic in range(NIC):
                for ot in range(4):
                    o0 = 2048 * ho + 512 * ot
                    mms.append((lambda e, ic=ic, ot=ot, o0=o0: e.matmul(
                        p["pb%d" % ot][:], g["xqT"][:, 128 * ic:128 * ic + 128],
                        W[:, 4096 * ic + o0:4096 * ic + o0 + 512],
                        start=(ic == 0), stop=False),
                        (wxt + wfree + self.wffn) if (ic == 0 and ot == 0) else ()))
            bb = 0 if ho == 0 else 32
            for ot in range(4):
                mms.append((lambda e, ot=ot, bb=bb: e.matmul(
                    p["pb%d" % ot][:], g["onesq_b"][bb:bb + 1, :],
                    g["brow"][bb:bb + 1, 512 * ot:512 * ot + 512],
                    start=False, stop=True), ()))
            for i, (f, w) in enumerate(mms):
                self.emit("tensor", f, wait=w, inc="tn" if i == len(mms) - 1 else None)
            if ho == 1:
                self.hzset("xqT_tn", "tn")
            wmm = (("tn", self.now("tn")),)
            for ot in range(4):
                self.emit("scalar", lambda e, ho=ho, ot=ot: e.activation(
                    g["sA"][:, 2048 * ho + 512 * ot:2048 * ho + 512 * ot + 512],
                    p["pb%d" % ot][:], Act.Gelu),
                    wait=wmm if ot == 0 else (), inc="sc" if ot == 3 else None)
            for ot in range(4):
                self.set_free("pb%d" % ot, "sc")
        wh = (("sc", self.now("sc")),)
        self.emit("vector", lambda e: e.reduce_max(out=fc[:, 18:19], in_=g["sA"][:, 0:2048],
                                                   axis=AX, apply_absolute_value=True), wait=wh)
        self.emit("vector", lambda e: e.reduce_max(out=fc[:, 19:20], in_=g["sA"][:, 2048:4096],
                                                   axis=AX, apply_absolute_value=True), inc="ve")
        self.emit("scalar", lambda e: e.activation(g["sC"][:, 0:2048], g["sA"][:, 0:2048],
                                                   Act.Square), wait=wh, inc="sc")
        self.emit("vector", lambda e: e.tensor_reduce(out=fc[:, 16:17], in_=g["sC"][:, 0:2048],
                                                      axis=AX, op=Alu.add),
                  wait=(("sc", self.now("sc")),), inc="ve")
        self.emit("scalar", lambda e: e.activation(g["sC"][:, 0:2048], g["sA"][:, 2048:4096],
                                                   Act.Square), wait=(("ve", self.now("ve")),), inc="sc")
        self.emit("vector", lambda e: e.tensor_reduce(out=fc[:, 17:18], in_=g["sC"][:, 0:2048],
                                                      axis=AX, op=Alu.add),
                  wait=(("sc", self.now("sc")),), inc="ve")
        self.vchain([
            lambda e: e.tensor_tensor(out=fc[:, 16:17], in0=fc[:, 16:17], in1=fc[:, 17:18], op=Alu.add),
            lambda e: e.tensor_scalar(out=fc[:, 20:21], in0=fc[:, 16:17], scalar1=1.0 / FFN_H,
                                      scalar2=1e-6, op0=Alu.mult, op1=Alu.add),
        ], wait0=(("ve", self.now("ve")),))
        self.emit("scalar", lambda e: e.activation(fc[:, 21:22], fc[:, 20:21], Act.Sqrt),
                  wait=(("ve", self.now("ve")),), inc="sc")
        self.vchain([
            lambda e: e.reciprocal(out=fc[:, 22:23], in_=fc[:, 21:22]),
            lambda e: e.tensor_tensor(out=fc[:, 18:19], in0=fc[:, 18:19], in1=fc[:, 19:20], op=Alu.max),
            lambda e: e.tensor_tensor(out=fc[:, 18:19], in0=fc[:, 18:19], in1=fc[:, 22:23], op=Alu.mult),
            lambda e: e.tensor_scalar_max(out=fc[:, 18:19], in0=fc[:, 18:19], scalar1=1e-5),
            lambda e: e.reciprocal(out=fc[:, 23:24], in_=fc[:, 18:19]),
            lambda e: e.tensor_scalar_mul(out=fc[:, 23:24], in0=fc[:, 23:24], scalar1=127.0),
            lambda e: e.tensor_tensor(out=fc[:, 24:25], in0=fc[:, 22:23], in1=fc[:, 23:24], op=Alu.mult),
            lambda e: e.tensor_tensor(out=fc[:, 25:26], in0=fc[:, 18:19], in1=g["fcol"][:, 41:42],
                                      op=Alu.mult),
            lambda e: e.tensor_scalar_mul(out=fc[:, 25:26], in0=fc[:, 25:26], scalar1=1.0 / 127.0),
            lambda e: e.tensor_tensor(out=fc[:, 26:27], in0=fc[:, 25:26], in1=g["jcol"][:], op=Alu.mult),
            lambda e: e.tensor_tensor(out=fc[:, 27:28], in0=fc[:, 25:26], in1=g["icol"][:], op=Alu.mult),
        ], wait0=(("sc", self.now("sc")),))
        wc2 = (("ve", self.now("ve")),)
        hqT = g["hqT%d" % slot]
        for hh in range(2):
            sl = slice(2048 * hh, 2048 * hh + 2048)
            self.emit("scalar", lambda e, sl=sl: e.activation(g["sB"][:, 0:2048], g["sA"][:, sl],
                                                              Act.Identity, bias=0.0,
                                                              scale=fc[:, 24:25]),
                      wait=(wc2 if hh == 0 else ()) + (("ve", self.now("ve")),), inc="sc")
            whb = (("tn", self.now("tn")),) if hh == 1 else ()
            self.vchain([
                lambda e: e.tensor_scalar(out=g["sC"][:, 0:2048], in0=g["sB"][:, 0:2048],
                                          scalar1=C32, scalar2=-C32, op0=Alu.add, op1=Alu.add),
                lambda e, sl=sl: e.tensor_scalar(out=g["sA"][:, sl], in0=g["sB"][:, 0:2048],
                                                 scalar1=C52, scalar2=-C52, op0=Alu.add, op1=Alu.add),
                lambda e, sl=sl: e.tensor_scalar_mul(out=g["sA"][:, sl], in0=g["sA"][:, sl],
                                                     scalar1=fc[:, 27:28]),
                lambda e, sl=sl: e.scalar_tensor_tensor(
                    out=g["hqb"][:], in0=g["sC"][:, 0:2048],
                    scalar=fc[:, 26:27], in1=g["sA"][:, sl], op0=Alu.mult, op1=Alu.add),
            ], wait0=(("sc", self.now("sc")),) + whb)
            for r in range(2):
                wtr = (("ve", self.now("ve")),) + self.free_wait("pbf")
                for i in range(8):
                    self.emit("tensor", lambda e, r=r, i=i: e.transpose(
                        p["pbf"][:, 128 * i:128 * i + 128],
                        g["hqb"][:, 1024 * r + 128 * i:1024 * r + 128 * i + 128],
                        g["ident"][:]), wait=wtr if i == 0 else (), inc="tn" if i == 7 else None)
                self.emit("vector", lambda e, r=r, hh=hh, hqT=hqT: e.tensor_copy(
                    out=hqT[:, 2048 * hh + 1024 * r:2048 * hh + 1024 * r + 1024], in_=p["pbf"][:]),
                    wait=(("tn", self.now("tn")),) +
                         ((self.hzw("mm2_tn") + self.hzw("ffn_dS"))
                          if (hh == 0 and r == 0) else ()), inc="ve")
                self.set_free("pbf", "ve")
        self.hz["hqT%d_ve" % slot] = ("ve", self.now("ve"))

    def ffn_pair(self, n2):
        for k in range(2):
            self.ffn_tile_h(n2 + k, k)
        self.ffn_mm2_pair(n2)

    def ffn_mm2_pair(self, n2):
        g = self.sb
        p = self.ps
        w2src = self.dram["w2q_hbm"].rearrange("(a p) o -> p a o", p=128)
        wh = (self.hz["hqT0_ve"], self.hz["hqT1_ve"])
        wfree = tuple(self.free_wait("pb%d" % b)[0] for b in range(4))
        buf_tok = [self.hz.get("mm2_tn", ("tn", 0))] * 2
        for hc in range(32):
            buf = g["wbuf"][:, 1024 * (hc % 2):1024 * (hc % 2) + 1024]
            qn, sm = ("sync", "dW2") if hc % 2 == 0 else ("gpsimd", "dW2g")
            self.emit(qn, lambda e, buf=buf, hc=hc: e.dma_start(out=buf, in_=w2src[:, hc, :]),
                      wait=(buf_tok[hc % 2],) + ((("%s" % sm, self.now(sm)),) if hc >= 2
                                                 else self.hzw("wq_dW")),
                      inc=sm, n=16)
            wd = ((sm, self.now(sm)),)
            mms = []
            for mt in range(2):
                hqT = g["hqT%d" % mt]
                for oh in range(2):
                    bk = "pb%d" % (2 * mt + oh)
                    mms.append((lambda e, hc=hc, bk=bk, oh=oh, buf=buf, hqT=hqT: e.matmul(
                        p[bk][:], hqT[:, 128 * hc:128 * hc + 128],
                        buf[:, 512 * oh:512 * oh + 512],
                        start=(hc == 0), stop=False),
                        (wd + ((wh + wfree) if hc == 0 else ())) if (mt == 0 and oh == 0) else ()))
            for i, (f, w) in enumerate(mms):
                self.emit("tensor", f, wait=w, inc="tn" if i == len(mms) - 1 else None)
            buf_tok[hc % 2] = ("tn", self.now("tn"))
        mms = []
        for mt in range(2):
            for oh in range(2):
                bk = "pb%d" % (2 * mt + oh)
                mms.append((lambda e, bk=bk, oh=oh: e.matmul(
                    p[bk][:], g["onesq_b"][64:65, :], g["brow"][64:65, 512 * oh:512 * oh + 512],
                    start=False, stop=True), ()))
        for i, (f, w) in enumerate(mms):
            self.emit("tensor", f, wait=w, inc="tn" if i == len(mms) - 1 else None)
        self.hzset("mm2_tn", "tn")
        wmm2 = (("tn", self.now("tn")),)
        for mt in range(2):
            resc = g["res"][:, C * (n2 + mt):C * (n2 + mt) + C]
            for oh in range(2):
                self.emit("vector", lambda e, oh=oh, mt=mt, resc=resc: e.tensor_add(
                    out=resc[:, 512 * oh:512 * oh + 512], in0=resc[:, 512 * oh:512 * oh + 512],
                    in1=p["pb%d" % (2 * mt + oh)][:]),
                    wait=wmm2 if (mt == 0 and oh == 0) else (),
                    inc="ve" if (mt == 1 and oh == 1) else None)
        for b in range(4):
            self.set_free("pb%d" % b, "ve")

    def phase_attn_b(self):
        g = self.sb
        SCB = float(64.0 ** -0.25)
        wprev = (("tn", self.now("tn")), ("sc", self.now("sc")), ("ve", self.now("ve")))
        self.hz.pop("wl_0", None)
        self.hz.pop("wl_1", None)
        for wi, (nm, s) in enumerate([("wq_t", SCB), ("wkb_t", SCB),
                                      ("wvb_t", None), ("wbo_t", None)]):
            self.load_warena_quadrant(nm, wi, scale=s)
        self.wb_loaded = (("ve", self.now("ve")),)
        self.emit("sync", lambda e: e.dma_start(out=g["sD"][0:1, 0:1024], in_=self.dram["bq_row"]),
                  wait=wprev + (("ve", self.now("ve")),), inc="dW", n=16)
        self.emit("sync", lambda e: e.dma_start(out=g["sD"][0:1, 1024:2048],
                                                in_=self.dram["bo_row"]), inc="dW", n=16)
        self.emit("sync", lambda e: e.dma_start(out=g["sD"][32:33, 0:1024],
                                                in_=self.dram["bk_row"]), inc="dW", n=16)
        self.emit("sync", lambda e: e.dma_start(out=g["sD"][64:65, 0:1024],
                                                in_=self.dram["bv_row"]), inc="dW", n=16)
        wb = (("dW", self.now("dW")),)
        self.emit("vector", lambda e: e.tensor_scalar_mul(out=g["brow_b"][0:1, 0:1024],
                                                          in0=g["sD"][0:1, 0:1024], scalar1=SCB),
                  wait=wb + wprev)
        self.emit("vector", lambda e: e.tensor_copy(out=g["brow_b"][0:1, 1024:2048],
                                                    in_=g["sD"][0:1, 1024:2048]))
        self.emit("vector", lambda e: e.tensor_scalar_mul(out=g["brow_b"][32:33, 0:1024],
                                                          in0=g["sD"][32:33, 0:1024], scalar1=SCB))
        self.emit("vector", lambda e: e.tensor_copy(out=g["brow_b"][64:65, 0:1024],
                                                    in_=g["sD"][64:65, 0:1024]), inc="ve")
        self.emit("vector", lambda e: e.memset(g["projT2"][:], 0.0), wait=wprev, inc="ve")
        self.emit("sync", lambda e: e.dma_start(out=g["wscr"][0:64, 0:128], in_=self.dram["proj_t"]),
                  wait=wprev + (("ve", self.now("ve")),), inc="dC", n=16)
        self.emit("sync", lambda e: e.dma_start(out=g["wscr"][64:128, 0:128],
                                                in_=self.dram["proj_t"]), inc="dC", n=16)
        wpj = (("dC", self.now("dC")),)
        self.emit("vector", lambda e: e.tensor_copy(out=g["projT2"][0:64, 0:128],
                                                    in_=g["wscr"][0:64, 0:128]),
                  wait=wpj + (("ve", self.now("ve")),), inc="ve")
        self.emit("vector", lambda e: e.tensor_copy(out=g["projT2"][64:128, 128:256],
                                                    in_=g["wscr"][64:128, 0:128]), inc="ve")
        self.emit("vector", lambda e: e.memset(g["vc65_b"][:], 1.0), wait=wprev, inc="ve")
        self.emit("vector", lambda e: e.memset(g["sA"][:, 0:1040], 0.0), inc="ve")
        self.emit("vector", lambda e: e.memset(g["Sb16_b"][:], 0.0), inc="ve")
        self.hzset("sb16_ve", "ve")
        self.hzset("projT2_ve", "ve")
        self.ln_stats(wait0=(("ve", self.now("ve")),))
        for t in (["qb_tn", "abT_tn", "vc65_ve", "qfT_ve", "kfT_ve", "kfb_ve"] +
                  ["ndf%d" % i for i in range(8)] + ["kvf%d" % i for i in range(8)] +
                  ["ndb%d" % i for i in range(2)] +
                  ["atf%d" % i for i in range(2)] + ["atmf%d" % i for i in range(2)]):
            self.hz.pop(t, None)
        for c in range(getattr(self, "b_nchunks", NCH)):
            self.attn_b_chunk(c)
        self.tap("res_b", g["res"][:], wait=(("ve", self.now("ve")),))

    def attn_b_side(self, side):
        g = self.sb
        p = self.ps
        xb = g["qb_b" if side == 0 else "kb_b"]
        xT = g["qbT_b" if side == 0 else "kbT_b"]
        fb = g["qfb_b" if side == 0 else "kfb_b"]
        bc0 = 0 if side == 0 else 16
        wtr = (("sc", self.now("sc")),) + self.free_wait("pbf")
        for i in range(NIC):
            self.emit("tensor", lambda e, i=i, xb=xb: e.transpose(
                p["pbf"][:, 128 * i:128 * i + 128], xb[:, 128 * i:128 * i + 128],
                g["ident"][:]), wait=wtr if i == 0 else (), inc="tn" if i == NIC - 1 else None)
        if side == 0:
            self.hzset("qb_tn", "tn")
        self.emit("vector", lambda e, xT=xT: e.tensor_copy(out=xT[:], in_=p["pbf"][:]),
                  wait=(("tn", self.now("tn")),) + self.hzw("abT_tn"), inc="ve")
        self.set_free("pbf", "ve")
        if getattr(self, "b_substop", None) == "xT":
            return
        wut0 = (("ve", self.now("ve")),) + self.hzw("projT2_ve")
        for j in range(8):
            ub = p["pb%d" % (5 + j % 2)]
            self.emit("tensor", lambda e, j=j, ub=ub, xT=xT: e.matmul(
                ub[:, 0:256], xT[:, 128 * j:128 * j + 128],
                g["projT2"][:], start=True, stop=True),
                wait=(wut0 if j == 0 else ()) + self.hzw("u_f%d" % (j % 2)), inc="tn")
            wexp = (("tn", self.now("tn")),) + ((("ve", self.bcol_ve),) if j == 0 else ())
            if getattr(self, "b_substop", None) == "umm":
                self.hz["u_f%d" % (j % 2)] = ("tn", self.now("tn"))
                continue
            for k in range(2):
                self.emit("scalar", lambda e, j=j, k=k, ub=ub, fb=fb, bc0=bc0: e.activation(
                    fb[:, 256 * j + 128 * k:256 * j + 128 * k + 128],
                    ub[:, 128 * k:128 * k + 256 - 128], Act.Exp,
                    bias=g["fcol"][:, bc0 + 2 * j + k:bc0 + 2 * j + k + 1], scale=1.0),
                    wait=wexp if k == 0 else (), inc="sc" if k == 1 else None)
            self.hz["u_f%d" % (j % 2)] = ("sc", self.now("sc"))
        if getattr(self, "b_substop", None) in ("umm", "exp"):
            return
        self.emit("vector", lambda e, fb=fb: e.tensor_scalar_add(out=fb[:], in0=fb[:],
                                                                 scalar1=1e-6),
                  wait=(("sc", self.now("sc")),), inc="ve")
        if side == 1:
            self.hzset("kfb_ve", "ve")

    def attn_b_side_T(self, side):
        g = self.sb
        p = self.ps
        fb = g["qfb_b" if side == 0 else "kfb_b"]
        fT = g["qfT_b" if side == 0 else "kfT_b"]
        for r in range(2):
            wtr = (("ve", self.now("ve")),) + self.free_wait("pbf")
            for i in range(NIC):
                self.emit("tensor", lambda e, r=r, i=i, fb=fb: e.transpose(
                    p["pbf"][:, 128 * i:128 * i + 128],
                    fb[:, 1024 * r + 128 * i:1024 * r + 128 * i + 128],
                    g["ident"][:]), wait=wtr if i == 0 else (), inc="tn" if i == NIC - 1 else None)
            self.emit("vector", lambda e, r=r, fT=fT: e.tensor_copy(
                out=fT[:, 1024 * r:1024 * r + 1024], in_=p["pbf"][:]),
                wait=(("tn", self.now("tn")),), inc="ve")
            self.set_free("pbf", "ve")
        self.hzset("qfT_ve" if side == 0 else "kfT_ve", "ve")

    def attn_b_head_pre(self, h, c):
        g = self.sb
        p = self.ps
        nd = p["pb%d" % (2 + h % 2)][:, 128 * ((h // 2) % 4):128 * ((h // 2) % 4) + 65]
        w0 = (self.hzw("qfT_ve") + self.hzw("sb16_ve")) if h == 0 else ()
        self.emit("tensor", lambda e, h=h, nd=nd: e.matmul(
            nd, g["qfT_b"][:, 128 * h:128 * h + 128], g["Sb16_b"][:, 65 * h:65 * h + 65],
            start=True, stop=False),
            wait=w0 + self.hzw("ndf%d" % (h % 8)) + self.hzw("ndb%d" % (h % 2)))
        pa = p["pb%d" % (h % 2)][:, 0:128]
        self.emit("tensor", lambda e, h=h, pa=pa: e.matmul(
            pa, g["kfT_b"][:, 128 * h:128 * h + 128], g["qfT_b"][:, 128 * h:128 * h + 128],
            start=True, stop=True),
            wait=(self.hzw("kfT_ve") if h == 0 else ()) + self.hzw("atf%d" % (h % 2)), inc="tn")
        self.hz["at_tn%d" % (h % 2)] = ("tn", self.now("tn"))

    def attn_b_head_mask(self, h):
        g = self.sb
        p = self.ps
        pa = p["pb%d" % (h % 2)][:, 0:128]
        sl = g["sB"][:, 128 * (h % 2):128 * (h % 2) + 128]
        self.emit("vector", lambda e, pa=pa, sl=sl: e.tensor_mul(out=sl, in0=pa,
                                                                 in1=g["mask_ec"][:]),
                  wait=(self.hz["at_tn%d" % (h % 2)],), inc="ve")
        self.hz["atf%d" % (h % 2)] = ("ve", self.now("ve"))
        self.emit("vector", lambda e, h=h, sl=sl: e.tensor_copy(
            out=g["atm_b"][h % 2][:], in_=sl),
            wait=self.hzw("atmf%d" % (h % 2)) + (("ve", self.now("ve")),), inc="ve")
        self.hz["atm_ve%d" % (h % 2)] = ("ve", self.now("ve"))

    def attn_b_head_post(self, h, c):
        g = self.sb
        p = self.ps
        nd = p["pb%d" % (2 + h % 2)][:, 128 * ((h // 2) % 4):128 * ((h // 2) % 4) + 65]
        kv = p["pb%d" % (4 + (h % 8) // 4)][:, 128 * (h % 4):128 * (h % 4) + 65]
        self.emit("tensor", lambda e, h=h, nd=nd: e.matmul(
            nd, g["atm_b"][h % 2][:], g["vc65_b"][:, 65 * h:65 * h + 65],
            start=False, stop=True),
            wait=(self.hz["atm_ve%d" % (h % 2)],) +
                 (self.hzw("vc65_ve") if h == 0 else ()), inc="tn")
        self.hz["atmf%d" % (h % 2)] = ("tn", self.now("tn"))
        ndtn = ("tn", self.now("tn"))
        self.emit("tensor", lambda e, h=h, kv=kv: e.matmul(
            kv, g["kfb_b"][:, 128 * h:128 * h + 128], g["vc65_b"][:, 65 * h:65 * h + 65],
            start=True, stop=True),
            wait=(self.hzw("kfb_ve") if h == 0 else ()) + self.hzw("kvf%d" % (h % 8)), inc="tn")
        self.emit("vector", lambda e, h=h, kv=kv: e.tensor_add(
            out=g["sA"][:, 65 * h:65 * h + 65], in0=g["sA"][:, 65 * h:65 * h + 65], in1=kv),
            wait=(("tn", self.now("tn")),), inc="ve")
        self.hz["kvf%d" % (h % 8)] = ("ve", self.now("ve"))
        dc, rc = 42 + (h % 2), 44 + (h % 2)
        nsl = g["sB"][:, 1024 + 128 * (h % 2):1024 + 128 * (h % 2) + 65]
        self.emit("vector", lambda e, nd=nd, nsl=nsl: e.tensor_copy(out=nsl, in_=nd),
                  wait=(ndtn,), inc="ve")
        self.hz["ndf%d" % (h % 8)] = ("ve", self.now("ve"))
        self.hz["ndb%d" % (h % 2)] = ("ve", self.now("ve"))
        self.vchain([
            lambda e, nsl=nsl, dc=dc: e.tensor_scalar_add(
                out=g["fcol"][:, dc:dc + 1], in0=nsl[:, 64:65], scalar1=1e-6),
            lambda e, dc=dc, rc=rc: e.reciprocal(out=g["fcol"][:, rc:rc + 1],
                                                 in_=g["fcol"][:, dc:dc + 1]),
            lambda e, h=h, nsl=nsl, rc=rc: e.tensor_scalar_mul(
                out=g["ob_b"][:, 64 * h:64 * h + 64], in0=nsl[:, 0:64],
                scalar1=g["fcol"][:, rc:rc + 1]),
        ], wait0=(("ve", self.now("ve")),))

    def attn_b_chunk(self, c):
        g = self.sb
        p = self.ps
        LNM = float(np.log(np.sqrt(np.float32(128.0))))
        resc = g["res"][:, C * c:C * c + C]
        w0 = (self.stats_done + self.wb_loaded + (("tn", self.now("tn")),)) if c == 0 else ()
        self.emit("scalar", lambda e, c=c: e.activation(
            g["xlnb_b"][:], resc, Act.Identity, bias=g["bcol"][:, c:c + 1],
            scale=g["rcol"][:, c:c + 1]), wait=w0 + self.hzw("qb_tn"), inc="sc")
        wtp = (("sc", self.now("sc")),) + self.free_wait("pbf")
        for i in range(NIC):
            self.emit("tensor", lambda e, i=i: e.transpose(
                p["pbf"][:, 128 * i:128 * i + 128], g["xlnb_b"][:, 128 * i:128 * i + 128],
                g["ident"][:]), wait=wtp if i == 0 else (), inc="tn" if i == NIC - 1 else None)
        self.emit("vector", lambda e: e.tensor_copy(out=g["xnT_b"][:], in_=p["pbf"][:]),
                  wait=(("tn", self.now("tn")),) + self.hzw("abT_tn"), inc="ve")
        self.set_free("pbf", "ve")
        if getattr(self, "b_stop", None) == "xn":
            return
        first = True
        for wi in range(3):
            br = (0, 32, 64)[wi]
            for oh in range(2):
                bank = p["pb%d" % (2 * wi + oh)]
                for ic in range(NIC):
                    self.emit("tensor", lambda e, wi=wi, oh=oh, ic=ic, bank=bank: e.matmul(
                        bank[:], g["xnT_b"][:, 128 * ic:128 * ic + 128],
                        g["warena"][:, 8192 * wi + 1024 * ic + 512 * oh:
                                    8192 * wi + 1024 * ic + 512 * oh + 512],
                        start=(ic == 0), stop=False),
                        wait=(("ve", self.now("ve")),) if first else ())
                    first = False
                self.emit("tensor", lambda e, br=br, oh=oh, bank=bank: e.matmul(
                    bank[:], g["onesq_b"][br:br + 1, :],
                    g["brow_b"][br:br + 1, 512 * oh:512 * oh + 512],
                    start=False, stop=True), inc="tn")
        wmm = (("tn", self.now("tn")),)
        if getattr(self, "b_stop", None) == "qkv":
            return
        for i, (bank, dst) in enumerate([(p["pb0"], g["qb_b"][:, 0:512]),
                                         (p["pb1"], g["qb_b"][:, 512:1024]),
                                         (p["pb2"], g["kb_b"][:, 0:512]),
                                         (p["pb3"], g["kb_b"][:, 512:1024]),
                                         (p["pb4"], g["ob_b"][:, 0:512]),
                                         (p["pb5"], g["ob_b"][:, 512:1024])]):
            self.emit("scalar", lambda e, bank=bank, dst=dst: e.activation(
                dst, bank[:], Act.Identity, bias=0.0, scale=1.0),
                wait=wmm if i == 0 else (), inc="sc")
        self.emit("vector", lambda e: e.tensor_copy(
            out=g["vc65_b"][:].rearrange("p (gp d) -> p gp d", d=65)[:, :, 0:64],
            in_=g["ob_b"][:].rearrange("p (gp d) -> p gp d", d=64)),
            wait=(("sc", self.now("sc")),), inc="ve")
        self.hzset("vc65_ve", "ve")
        for side, xb in ((0, g["qb_b"]), (1, g["kb_b"])):
            self.emit("scalar", lambda e, xb=xb: e.activation(g["sB"][:, 0:1024], xb[:],
                                                              Act.Square),
                      wait=(("sc", self.now("sc")), ("ve", self.now("ve"))), inc="sc")
            self.emit("vector", lambda e, side=side: e.tensor_reduce(
                out=g["fcol"][:, 16 * side:16 * side + 16],
                in_=g["sB"][:, 0:1024].rearrange("p (gp d) -> p gp d", d=64),
                axis=AX, op=Alu.add), wait=(("sc", self.now("sc")),), inc="ve")
            self.emit("vector", lambda e, side=side, LNM=LNM: e.tensor_scalar(
                out=g["fcol"][:, 16 * side:16 * side + 16],
                in0=g["fcol"][:, 16 * side:16 * side + 16],
                scalar1=-0.5, scalar2=-LNM, op0=Alu.mult, op1=Alu.add),
                wait=(("ve", self.now("ve")),), inc="ve")
        self.bcol_ve = self.now("ve")
        if getattr(self, "b_stop", None) == "evac":
            return
        if c == 0:
            self.tap("b_qb", g["qb_b"][:], wait=(("sc", self.now("sc")),))
            self.tap("b_kb", g["kb_b"][:], wait=(("sc", self.now("sc")),))
            self.tap("b_vc65", g["vc65_b"][:], wait=(("ve", self.now("ve")),))
            self.tap("b_bcol", g["fcol"][:, 0:32], wait=(("ve", self.now("ve")),))
        self.attn_b_side(0)
        if c == 0:
            self.tap("b_qfb", g["qfb_b"][:], wait=(("ve", self.now("ve")),))
        self.attn_b_side(1)
        if c == 0:
            self.tap("b_kfb", g["kfb_b"][:], wait=(("ve", self.now("ve")),))
        if getattr(self, "b_stop", None) == "sideu":
            return
        self.attn_b_side_T(0)
        self.attn_b_side_T(1)
        if getattr(self, "b_stop", None) == "sideT":
            return
        nheads = getattr(self, "b_nheads", H)
        if nheads == 0:
            return
        self.attn_b_head_pre(0, c)
        self.attn_b_head_mask(0)
        for h in range(nheads):
            if h + 1 < nheads:
                self.attn_b_head_pre(h + 1, c)
            self.attn_b_head_post(h, c)
            if h + 1 < nheads:
                self.attn_b_head_mask(h + 1)
        if c == 0:
            self.tap("b_ob", g["ob_b"][:], wait=(("ve", self.now("ve")),))
            self.tap("b_S", g["sA"][:, 0:1040], wait=(("ve", self.now("ve")),))
        self.emit("vector", lambda e: e.tensor_copy(out=g["Sb16_b"][:], in_=g["sA"][:, 0:1040]),
                  wait=(("ve", self.now("ve")),), inc="ve")
        self.hzset("sb16_ve", "ve")
        if getattr(self, "b_stop", None) == "heads":
            return
        wtr = (("ve", self.now("ve")),) + self.free_wait("pbf")
        for i in range(NIC):
            self.emit("tensor", lambda e, i=i: e.transpose(
                p["pbf"][:, 128 * i:128 * i + 128], g["ob_b"][:, 128 * i:128 * i + 128],
                g["ident"][:]), wait=wtr if i == 0 else (), inc="tn" if i == NIC - 1 else None)
        self.emit("vector", lambda e: e.tensor_copy(out=g["obT_b"][:], in_=p["pbf"][:]),
                  wait=(("tn", self.now("tn")),), inc="ve")
        self.set_free("pbf", "ve")
        wo0 = (("ve", self.now("ve")),) + self.hzw("atf0") + self.hzw("atf1")
        for oh in range(2):
            bank = p["pb%d" % oh]
            for ic in range(NIC):
                self.emit("tensor", lambda e, oh=oh, ic=ic, bank=bank: e.matmul(
                    bank[:], g["obT_b"][:, 128 * ic:128 * ic + 128],
                    g["warena"][:, 24576 + 1024 * ic + 512 * oh:24576 + 1024 * ic + 512 * oh + 512],
                    start=(ic == 0), stop=False),
                    wait=wo0 if (oh == 0 and ic == 0) else ())
            self.emit("tensor", lambda e, oh=oh, bank=bank: e.matmul(
                bank[:], g["onesq_b"][0:1, :],
                g["brow_b"][0:1, 1024 + 512 * oh:1024 + 512 * oh + 512],
                start=False, stop=True), inc="tn")
        self.hzset("abT_tn", "tn")
        wres = (("tn", self.now("tn")),)
        for oh in range(2):
            self.emit("vector", lambda e, oh=oh, resc=resc: e.tensor_add(
                out=resc[:, 512 * oh:512 * oh + 512], in0=resc[:, 512 * oh:512 * oh + 512],
                in1=p["pb%d" % oh][:]), wait=wres if oh == 0 else (),
                inc="ve" if oh == 1 else None)

    def phase_store(self):
        g = self.sb
        yv = self.dram["y"].rearrange("(n p) d -> p n d", p=128)
        self.emit("sync", lambda e: e.dma_start(
            out=yv, in_=g["res"][:].rearrange("p (n d) -> p n d", n=NCH)),
            wait=(("ve", self.now("ve")), ("sc", self.now("sc")), ("tn", self.now("tn")),
                  ("gp", self.now("gp"))), inc="dO", n=16)
        self.emit("sync", lambda e: e.wait_ge(self.semh["dO"], self.now("dO")))
        if self.now("dT"):
            self.emit("sync", lambda e, v=self.now("dT"): e.wait_ge(self.semh["dT"], v))

    def finalize(self):
        blk = self.stack.enter_context(self.nc.Block())
        for ename in ENGS:
            dec = getattr(blk, ename)
            ops = list(self.q[ename])
            def body(eng, ops=ops):
                for f in ops:
                    f(eng)
            dec(body)


def build(stack, debug_taps=(), stop_after=None, nchunks=NCH, detect=False,
          b_stop=None, b_nchunks=NCH, b_nheads=16, b_substop=None):
    nc = bass.Bass("TRN2", target_bir_lowering=False, detect_race_conditions=detect)
    kb = KB(nc, stack, debug_taps=debug_taps, stop_after=stop_after)
    kb.nchunks = nchunks
    kb.b_stop = b_stop
    kb.b_nchunks = b_nchunks
    kb.b_nheads = b_nheads
    kb.b_substop = b_substop
    kb.declare()
    kb.phase_load()
    if stop_after != "load":
        kb.phase_wquant()
        kb.phase_attn_a()
        if stop_after != "attn_a":
            kb.phase_ffn(1)
            if stop_after != "ffn1":
                kb.phase_attn_b()
                if stop_after != "attn_b":
                    kb.phase_ffn(2)
    kb.phase_store()
    kb.finalize()
    return nc, kb


from concourse.bass_utils import run_bass_kernel_spmd
import contextlib


def kernel(**inputs):
    X = np.ascontiguousarray(np.asarray(inputs["L0_attn_a_x"], np.float32))
    hm = host_inputs({k: np.asarray(v, np.float32) for k, v in inputs.items()})
    with contextlib.ExitStack() as stack:
        nc, kb = build(stack)
        in_maps = []
        for core in range(8):
            m = dict(hm)
            m["x"] = np.ascontiguousarray(X[core])
            in_maps.append(m)
        r = run_bass_kernel_spmd(nc, in_maps, core_ids=list(range(8)))
        out = np.stack([np.asarray(r.results[core]["y"], np.float32)
                        for core in range(8)], axis=0)
    return out


# version 04 (final)
# speedup vs baseline: 1.0308x; 1.0308x over previous
import numpy as np
import ml_dtypes
import concourse.bass as bass
import concourse.mybir as mybir

F32 = mybir.dt.float32
BF16 = mybir.dt.bfloat16
Act = mybir.ActivationFunctionType
Alu = mybir.AluOpType
AX = mybir.AxisListType.X
bfdt = ml_dtypes.bfloat16

T, C, H, DK = 2048, 1024, 16, 64
NCH, NIC = 16, 8
FFN_H = 4096
C32 = 1.5 * 2.0 ** 23
C52 = 1.5 * 2.0 ** 52
LNSM = float(np.log(np.sqrt(128.0)))
DKS = float(DK ** -0.25)

ENGS = ("sync", "tensor", "vector", "scalar", "gpsimd")


def host_constants():
    ii = np.arange(128, dtype=np.float64)
    cdt = ii[None, :] - 1.0 - ii[:, None]
    cdt = np.where(cdt >= 0, cdt, 1e30).astype(np.float32)
    cc = np.arange(NCH, dtype=np.float64)
    cd2 = (cc[None, :] - 1.0 - cc[:, None]) * 128.0
    cd2 = np.where(cd2 >= 0, cd2, 1e30).astype(np.float32)
    return {
        "c_ident": np.eye(128).astype(bfdt),
        "c_cdt": cdt,
        "c_cd2": cd2,
        "c_iota_row": np.broadcast_to(np.arange(128, dtype=np.float32), (16, 128)).copy(),
        "c_rev127": np.broadcast_to((127.0 - np.arange(128, dtype=np.float32))[:, None], (128, 16)).copy(),
        "c_mask_ec": np.tril(np.ones((128, 128), np.float32)).T.copy(),
        "c_ones_row": np.ones((1, 128), dtype=bfdt),
        "c_ones_col": np.ones((128, 1), dtype=np.float32),
        "c_onesq_b": np.ones((128, 128), dtype=bfdt),
        "c_onesq_f": np.ones((128, 128), dtype=np.float32),
    }


def host_inputs(inp):
    A = np.ascontiguousarray
    f = np.float32
    d = {}
    d["wk_t"] = A(inp["L0_attn_a_Wk"].T.astype(f))
    d["wv_t"] = A(inp["L0_attn_a_Wv"].T.astype(f))
    d["wr_t"] = A(inp["L0_attn_a_Wr"].T.astype(f))
    d["wao_t"] = A(inp["L0_attn_a_Wo"].T.astype(f))
    d["mixk_d"] = A(inp["L0_attn_a_mix_k"].reshape(NIC, 128).T.astype(f))
    d["mixv_d"] = A(inp["L0_attn_a_mix_v"].reshape(NIC, 128).T.astype(f))
    d["mixr_d"] = A(inp["L0_attn_a_mix_r"].reshape(NIC, 128).T.astype(f))
    d["td_rep"] = np.broadcast_to(inp["L0_attn_a_time_decay"].astype(f), (128, 16)).copy()
    d["tf_rep"] = np.broadcast_to(inp["L0_attn_a_time_first"].astype(f), (128, 16)).copy()
    d["td_row16"] = A(inp["L0_attn_a_time_decay"].astype(f)[:, None])
    d["wq_t"] = A(inp["L0_attn_b_Wq"].T.astype(f))
    d["wkb_t"] = A(inp["L0_attn_b_Wk"].T.astype(f))
    d["wvb_t"] = A(inp["L0_attn_b_Wv"].T.astype(f))
    d["wbo_t"] = A(inp["L0_attn_b_Wo"].T.astype(f))
    d["bq_row"] = A(inp["L0_attn_b_bq"].astype(f)[None, :])
    d["bk_row"] = A(inp["L0_attn_b_bk"].astype(f)[None, :])
    d["bv_row"] = A(inp["L0_attn_b_bv"].astype(f)[None, :])
    d["bo_row"] = A(inp["L0_attn_b_bo"].astype(f)[None, :])
    d["proj_t"] = A(inp["L0_attn_b_proj"].T.astype(f))
    d["w1_t"] = A(inp["L0_ffn_w1"].T.astype(f))
    d["w2_t"] = A(inp["L0_ffn_w2"].T.astype(f))
    d["b1_row"] = A(inp["L0_ffn_b1"].astype(f)[None, :])
    d["b2_row"] = A(inp["L0_ffn_b2"].astype(f)[None, :])
    d.update(host_constants())
    return d


class KB:
    def __init__(self, nc, stack, debug_taps=(), stop_after=None):
        self.nc = nc
        self.stack = stack
        self.q = {e: [] for e in ENGS}
        self.semh = {}
        self.semv = {}
        self.taps = set(debug_taps)
        self.stop_after = stop_after
        self.dram = {}
        self.sb = {}
        self.ps = {}
        self.bankfree = {}
        self.hz = {}
        self.n_tap = 0

    def sem(self, name):
        if name not in self.semh:
            self.semh[name] = self.stack.enter_context(self.nc.semaphore(name))
            self.semv[name] = 0
        return self.semh[name]

    def emit(self, eng, fn, wait=(), inc=None, n=1):
        w = tuple((s, v) for (s, v) in wait if v > 0)
        for s, _ in w:
            self.sem(s)
        if w:
            def fw(e, w=w):
                for s, v in w:
                    e.wait_ge(self.semh[s], v)
            self.q[eng].append(fw)
        if inc is not None:
            self.sem(inc)
            self.semv[inc] += n
            def fi(e, fn=fn, inc=inc, n=n):
                fn(e).then_inc(self.semh[inc], n)
            self.q[eng].append(fi)
        else:
            self.q[eng].append(fn)

    def now(self, sem):
        self.sem(sem)
        return self.semv[sem]

    def sbuf(self, name, shape, dtype):
        t = self.stack.enter_context(self.nc.sbuf_tensor(name, list(shape), dtype))
        self.sb[name] = t
        return t

    def psum(self, name, shape, dtype=F32):
        t = self.stack.enter_context(self.nc.psum_tensor(name, list(shape), dtype))
        self.ps[name] = t
        return t

    def din(self, name, shape, dtype):
        ap = self.nc.dram_tensor(name, list(shape), dtype, kind="ExternalInput").ap()
        self.dram[name] = ap
        return ap

    def dout(self, name, shape, dtype):
        ap = self.nc.dram_tensor(name, list(shape), dtype, kind="ExternalOutput").ap()
        self.dram[name] = ap
        return ap

    def free_wait(self, bank):
        return (self.bankfree.get(bank, ("ve", 0)),)

    def hzw(self, name):
        t = self.hz.get(name)
        return (t,) if t else ()

    def hzset(self, name, sem):
        self.hz[name] = (sem, self.now(sem))

    def set_free(self, bank, sem):
        self.bankfree[bank] = (sem, self.now(sem))

    def tap(self, name, src_ap, wait=()):
        if name not in self.taps:
            return
        shape = list(src_ap.shape)
        ap = self.dout("tap_" + name, shape, src_ap.dtype)
        self.emit("sync", lambda e, ap=ap, s=src_ap: e.dma_start(out=ap, in_=s),
                  wait=wait, inc="dT", n=16)

    def declare(self):
        self.x_in = self.din("x", [T, C], F32)
        self.y_out = self.dout("y", [T, C], F32)
        for n, s in [("wk_t", [C, C]), ("wv_t", [C, C]), ("wr_t", [C, C]), ("wao_t", [C, C]),
                     ("mixk_d", [128, 8]), ("mixv_d", [128, 8]), ("mixr_d", [128, 8]),
                     ("td_rep", [128, 16]), ("tf_rep", [128, 16]), ("td_row16", [16, 1]),
                     ("wq_t", [C, C]), ("wkb_t", [C, C]), ("wvb_t", [C, C]), ("wbo_t", [C, C]),
                     ("bq_row", [1, C]), ("bk_row", [1, C]), ("bv_row", [1, C]), ("bo_row", [1, C]),
                     ("proj_t", [64, 128]),
                     ("w1_t", [C, FFN_H]), ("w2_t", [FFN_H, C]),
                     ("b1_row", [1, FFN_H]), ("b2_row", [1, C]),
                     ("c_cdt", [128, 128]), ("c_cd2", [16, 16]), ("c_iota_row", [16, 128]),
                     ("c_rev127", [128, 16]), ("c_mask_ec", [128, 128]),
                     ("c_ones_col", [128, 1]), ("c_onesq_f", [128, 128])]:
            self.din(n, s, F32)
        self.din("c_ident", [128, 128], BF16)
        self.din("c_ones_row", [1, 128], BF16)
        self.din("c_onesq_b", [128, 128], BF16)
        self.dram["w1q_hbm"] = self.nc.dram_tensor("w1q_hbm", [C, FFN_H], BF16, kind="Internal").ap()
        self.dram["w2q_hbm"] = self.nc.dram_tensor("w2q_hbm", [FFN_H, C], BF16, kind="Internal").ap()

        sb = self.sbuf
        sb("res", [128, NCH * C], F32)
        sb("warena", [128, 32768], BF16)
        sb("ident", [128, 128], BF16)
        sb("mask_ec", [128, 128], F32)
        sb("cdt", [128, 128], F32)
        sb("onesrow", [1, 128], BF16)
        sb("onesq_b", [128, 128], BF16)
        sb("onesq_f", [128, 128], F32)
        sb("wbuf", [128, 2048], BF16)
        sb("fcol", [128, 48], F32)
        sb("icol", [128, 1], F32)
        sb("jcol", [128, 1], F32)
        sb("one_c", [128, 1], F32)
        sb("colscr", [128, 128], F32)
        sb("mcol", [128, NCH], F32)
        sb("rcol", [128, NCH], F32)
        sb("bcol", [128, NCH], F32)
        sb("statS", [128, NCH], F32)
        sb("statQ", [128, NCH], F32)
        sb("mixk", [128, 8], F32)
        sb("mixv", [128, 8], F32)
        sb("mixr", [128, 8], F32)
        sb("wscr", [128, 352], F32)
        sb("dtall", [128, 16 * 128], BF16)
        sb("dtall2", [128, 16 * 128], BF16)
        sb("d2all", [16, 16 * 16], BF16)
        sb("epow16", [16, 128], BF16)
        sb("wcar", [128, 16], BF16)
        sb("eucol", [128, 16], F32)
        sb("lwb", [128, 16], F32)
        sb("uallN", [16, C], BF16)
        sb("uallD", [16, 16], BF16)
        sb("sA", [128, 4096], F32)
        sb("sB", [128, 2048], F32)
        sb("sC", [128, 2048], F32)
        sb("arenaB", [128, 14368], BF16)
        aB = self.sb["arenaB"]
        self.sb["sD"] = self.sb["sC"][:, 0:2048]
        self.sb["junkb"] = self.sb["sB"][:, 0:1024]
        self.sb["xlnb"] = aB[:, 0:1024]
        self.sb["xkb"] = aB[:, 1024:2048]
        self.sb["xvb"] = aB[:, 2048:3072]
        self.sb["xrb"] = aB[:, 3072:4096]
        self.sb["pchunk"] = aB[:, 4096:5120]
        self.sb["sgr"] = aB[:, 5120:6144]
        self.sb["gatedT"] = aB[:, 6144:7168]
        self.sb["xlnTc"] = aB[:, 7168:8224]
        self.sb["uscr"] = aB[:, 8224:12320]
        self.sb["eprows2"] = aB[:, 12320:14368]
        self.sb["xqb"] = aB[:, 0:1024]
        self.sb["xqT"] = aB[:, 1024:2048]
        self.sb["hqb"] = aB[:, 2048:4096]
        self.sb["hqT0"] = aB[:, 4096:8192]
        self.sb["hqT1"] = aB[:, 8192:12288]
        self.sb["brow"] = aB[:, 12288:14336]
        self.sb["wq_bf"] = aB[:, 0:4096]
        self.sb["qbT_b"] = aB[:, 0:1024]
        self.sb["kbT_b"] = aB[:, 1024:2048]
        self.sb["qfT_b"] = aB[:, 0:2048]
        self.sb["qfb_b"] = aB[:, 2048:4096]
        self.sb["kfT_b"] = aB[:, 2048:4096]
        self.sb["kfb_b"] = aB[:, 4096:6144]
        self.sb["vc65_b"] = aB[:, 6144:7184]
        self.sb["atm_b"] = (aB[:, 7184:7312], aB[:, 7312:7440])
        self.sb["ob_b"] = aB[:, 7440:8464]
        self.sb["xlnb_b"] = aB[:, 8464:9488]
        self.sb["qb_b"] = aB[:, 8464:9488]
        self.sb["xnT_b"] = aB[:, 9488:10512]
        self.sb["kb_b"] = aB[:, 9488:10512]
        self.sb["obT_b"] = aB[:, 9488:10512]
        self.sb["projT2"] = aB[:, 10512:10768]
        self.sb["Sb16_b"] = aB[:, 11248:12288]
        self.sb["brow_b"] = aB[:, 12288:14336]
        sb("ekzb", [128, 16], BF16)
        for i in range(7):
            self.psum("pb%d" % i, [128, 512], F32)
        self.psum("pbf", [128, 1024], BF16)

    def phase_load(self):
        g = self.sb
        xv = self.dram["x"].rearrange("(n p) d -> p n d", p=128)
        self.emit("sync", lambda e: e.dma_start(
            out=g["res"][:].rearrange("p (n d) -> p n d", n=NCH), in_=xv), inc="dX", n=16)
        for sb_n, dr_n in [("ident", "c_ident"), ("mask_ec", "c_mask_ec"), ("cdt", "c_cdt"),
                           ("onesrow", "c_ones_row"), ("one_c", "c_ones_col"),
                           ("onesq_b", "c_onesq_b"), ("onesq_f", "c_onesq_f"),
                           ("mixk", "mixk_d"), ("mixv", "mixv_d"), ("mixr", "mixr_d")]:
            self.emit("sync", lambda e, a=g[sb_n], b=self.dram[dr_n]: e.dma_start(out=a[:], in_=b),
                      inc="dX", n=16)
        self.x_loaded = self.now("dX")
        self.emit("vector", lambda e: e.tensor_scalar(out=g["icol"][:], in0=g["one_c"][:],
                                                      scalar1=C52, scalar2=-C52, op0=Alu.add, op1=Alu.add),
                  wait=(("dX", self.x_loaded),), inc="ve")
        self.emit("vector", lambda e: e.tensor_scalar(out=g["jcol"][:], in0=g["icol"][:],
                                                      scalar1=-1.0, scalar2=1.0, op0=Alu.mult, op1=Alu.add),
                  wait=(("ve", self.now("ve")),), inc="ve")
        self.gadget_ready = (("ve", self.now("ve")),)

    def ln_stats(self, wait0=()):
        g = self.sb
        jb = (g["sB"][:, 0:1024], g["sB"][:, 1024:2048])
        for n in range(NCH):
            sl = g["res"][:, C * n:C * n + C]
            wq = (("ve", self.now("ve")),) if n >= 2 else ()
            self.emit("scalar", lambda e, s=sl, n=n: e.activation(
                jb[n % 2], s, Act.Square),
                wait=(wait0 if n == 0 else ()) + wq, inc="sc")
            self.emit("vector", lambda e, s=sl, n=n: e.tensor_reduce(
                out=g["statS"][:, n:n + 1], in_=s, axis=AX, op=Alu.add),
                wait=wait0 if n == 0 else ())
            self.emit("vector", lambda e, n=n: e.tensor_reduce(
                out=g["statQ"][:, n:n + 1], in_=jb[n % 2], axis=AX, op=Alu.add),
                wait=(("sc", self.now("sc")),), inc="ve")
        wv = (("sc", self.now("sc")), ("ve", self.now("ve")))
        cs = g["colscr"]
        self.emit("vector", lambda e: e.tensor_scalar_mul(out=g["mcol"][:, 0:NCH],
                                                          in0=g["statS"][:, 0:NCH], scalar1=1.0 / C),
                  wait=wv, inc="ve")
        self.emit("vector", lambda e: e.scalar_tensor_tensor(
            out=cs[:, 16:32], in0=g["mcol"][:, 0:NCH], scalar=-1.0,
            in1=g["mcol"][:, 0:NCH], op0=Alu.mult, op1=Alu.mult),
            wait=(("ve", self.now("ve")),), inc="ve")
        self.emit("vector", lambda e: e.scalar_tensor_tensor(
            out=cs[:, 0:16], in0=g["statQ"][:, 0:NCH], scalar=1.0 / C,
            in1=cs[:, 16:32], op0=Alu.mult, op1=Alu.add),
            wait=(("ve", self.now("ve")),), inc="ve")
        self.emit("vector", lambda e: e.tensor_scalar_add(out=cs[:, 0:16], in0=cs[:, 0:16],
                                                          scalar1=1e-5),
                  wait=(("ve", self.now("ve")),), inc="ve")
        self.emit("scalar", lambda e: e.activation(cs[:, 112:128], cs[:, 0:16], Act.Sqrt),
                  wait=(("ve", self.now("ve")),), inc="sc")
        self.emit("vector", lambda e: e.reciprocal(out=g["rcol"][:, 0:NCH], in_=cs[:, 112:128]),
                  wait=(("sc", self.now("sc")),), inc="ve")
        self.emit("vector", lambda e: e.scalar_tensor_tensor(
            out=g["bcol"][:, 0:NCH], in0=g["mcol"][:, 0:NCH], scalar=-1.0,
            in1=g["rcol"][:, 0:NCH], op0=Alu.mult, op1=Alu.mult),
            wait=(("ve", self.now("ve")),), inc="ve")
        self.stats_done = (("ve", self.now("ve")),)
        self.tap("statS", g["statS"][:], wait=self.stats_done)
        self.tap("statQ", g["statQ"][:], wait=self.stats_done)
        self.tap("rcol", g["rcol"][:], wait=self.stats_done)
        self.tap("bcol", g["bcol"][:], wait=self.stats_done)
        self.tap("mcol", g["mcol"][:], wait=self.stats_done)
        self.tap("vareps", g["colscr"][:, 0:16], wait=self.stats_done)
        self.tap("sqrtout", g["colscr"][:, 112:128], wait=self.stats_done)
        self.tap("m2col", g["colscr"][:, 16:32], wait=self.stats_done)

    def load_warena_quadrant(self, dram_name, wi, scale=None):
        g = self.sb
        src = self.dram[dram_name].rearrange("(a p) o -> p a o", p=128)
        for part in range(4):
            qn, sm = ("sync", "dW") if part % 2 == 0 else ("gpsimd", "dWg")
            buf = (g["sD"] if part % 2 == 0 else g["sB"])[:, 0:2048]
            tok = self.hz.get("wl_%d" % (part % 2))
            self.emit(qn, lambda e, src=src, part=part, buf=buf: e.dma_start(
                out=buf.rearrange("p (a o) -> p a o", o=C), in_=src[:, 2 * part:2 * part + 2, :]),
                wait=(("ve", self.now("ve")),) if tok is None else (tok,), inc=sm, n=16)
            dst = g["warena"][:, 8192 * wi + 2048 * part: 8192 * wi + 2048 * part + 2048]
            if scale is None:
                self.emit("vector", lambda e, dst=dst, buf=buf: e.tensor_copy(out=dst, in_=buf),
                          wait=((sm, self.now(sm)),), inc="ve")
            else:
                self.emit("vector", lambda e, dst=dst, buf=buf, s=scale: e.tensor_scalar_mul(
                    out=dst, in0=buf, scalar1=s), wait=((sm, self.now(sm)),), inc="ve")
            self.hz["wl_%d" % (part % 2)] = ("ve", self.now("ve"))

    def attn_a_consts(self):
        g = self.sb
        cs = g["wscr"]
        self.emit("sync", lambda e: e.dma_start(out=cs[:, 0:16], in_=self.dram["td_rep"]), inc="dC", n=16)
        self.emit("sync", lambda e: e.dma_start(out=cs[:, 16:32], in_=self.dram["tf_rep"]), inc="dC", n=16)
        self.emit("sync", lambda e: e.dma_start(out=cs[:, 32:48], in_=self.dram["c_rev127"]), inc="dC", n=16)
        self.emit("sync", lambda e: e.dma_start(out=cs[0:16, 48:49], in_=self.dram["td_row16"]), inc="dC", n=16)
        self.emit("sync", lambda e: e.dma_start(out=cs[0:16, 64:80], in_=self.dram["c_cd2"]), inc="dC", n=16)
        self.emit("sync", lambda e: e.dma_start(out=cs[0:16, 80:208], in_=self.dram["c_iota_row"]), inc="dC", n=16)
        wtd = (("dC", self.now("dC")),)
        self.emit("scalar", lambda e: e.activation(g["lwb"][:, 0:16], cs[:, 0:16], Act.Exp), wait=wtd)
        self.emit("scalar", lambda e: e.activation(g["eucol"][:, 0:16], cs[:, 16:32], Act.Exp), inc="sc")
        self.emit("vector", lambda e: e.tensor_scalar_mul(out=g["lwb"][:, 0:16], in0=g["lwb"][:, 0:16],
                                                          scalar1=-1.0), wait=(("sc", self.now("sc")),), inc="ve")
        wlw = (("ve", self.now("ve")),)
        for h in range(H):
            self.emit("scalar", lambda e, h=h: e.activation(
                g["dtall"][:, 128 * h:128 * h + 128], g["cdt"][:], Act.Exp,
                scale=g["lwb"][:, h:h + 1]), wait=wlw if h == 0 else (), inc="sc")
        self.emit("vector", lambda e: e.tensor_tensor(out=cs[:, 336:352], in0=cs[:, 32:48],
                                                      in1=g["lwb"][:, 0:16], op=Alu.mult), wait=wlw, inc="ve")
        self.emit("scalar", lambda e: e.activation(g["wcar"][:, 0:16], cs[:, 336:352], Act.Exp),
                  wait=(("ve", self.now("ve")),), inc="sc")
        self.emit("scalar", lambda e: e.activation(cs[0:16, 49:50], cs[0:16, 48:49], Act.Exp), wait=wtd, inc="sc")
        self.emit("vector", lambda e: e.tensor_scalar_mul(out=cs[0:16, 50:51], in0=cs[0:16, 49:50],
                                                          scalar1=-1.0), wait=(("sc", self.now("sc")),), inc="ve")
        wlw16 = (("ve", self.now("ve")),)
        for h in range(H):
            self.emit("scalar", lambda e, h=h: e.activation(
                g["d2all"][0:16, 16 * h:16 * h + 16], cs[0:16, 64:80], Act.Exp,
                scale=g["lwb"][0:16, h:h + 1]), wait=wlw16 if h == 0 else (), inc="sc")
        self.emit("scalar", lambda e: e.activation(cs[0:16, 208:336], cs[0:16, 80:208], Act.Exp,
                                                   scale=cs[0:16, 50:51]), wait=wlw16, inc="sc")
        self.emit("vector", lambda e: e.tensor_copy(out=g["epow16"][:], in_=cs[0:16, 208:336]),
                  wait=(("sc", self.now("sc")),), inc="ve")
        wdt2 = (("sc", self.now("sc")),)
        for h in range(H):
            self.emit("vector", lambda e, h=h: e.tensor_scalar_mul(
                out=g["sA"][:, 128 * h:128 * h + 128], in0=g["ident"][:],
                scalar1=g["eucol"][:, h:h + 1]),
                wait=wdt2 if h == 0 else (), inc="ve" if h == H - 1 else None)
        wdt3 = (("ve", self.now("ve")),)
        for h in range(H):
            self.emit("vector", lambda e, h=h: e.tensor_tensor(
                out=g["dtall2"][:, 128 * h:128 * h + 128], in0=g["dtall"][:, 128 * h:128 * h + 128],
                in1=g["sA"][:, 128 * h:128 * h + 128], op=Alu.add),
                wait=wdt3 if h == 0 else (), inc="ve" if h == H - 1 else None)
        self.emit("sync", lambda e: e.dma_start(out=g["eprows2"][32:33, :], in_=g["epow16"][:]),
                  wait=(("ve", self.now("ve")),), inc="dC", n=16)
        self.emit("sync", lambda e: e.dma_start(out=g["eprows2"][64:65, :], in_=g["epow16"][:]),
                  inc="dC", n=16)
        self.aconsts_done = (("dC", self.now("dC")), ("sc", self.now("sc")))

    def phase_attn_a(self):
        g = self.sb
        self.attn_a_consts()
        self.hz.pop("wl_0", None)
        self.hz.pop("wl_1", None)
        for wi, nm in enumerate(["wk_t", "wv_t", "wr_t", "wao_t"]):
            self.load_warena_quadrant(nm, wi)
        self.wa_loaded = (("ve", self.now("ve")),)
        self.ln_stats(wait0=(("dX", self.x_loaded),) + (("ve", self.now("ve")),))
        self.emit("vector", lambda e: e.memset(g["xlnTc"][:], 0.0), inc="ve")
        self.emit("vector", lambda e: e.memset(g["uallN"][:], 0.0), inc="ve")
        self.emit("vector", lambda e: e.memset(g["uallD"][:], 0.0), inc="ve")
        self.hz["init_ms"] = ("ve", self.now("ve"))
        for c in range(getattr(self, "nchunks", NCH)):
            self.attn_a_chunk(c)
        self.tap("res_a", g["res"][:], wait=(("ve", self.now("ve")),))
        self.tap("uallN", g["uallN"][:], wait=(("dS", self.now("dS")),))
        self.tap("uallD", g["uallD"][:], wait=(("dS", self.now("dS")),))
        self.tap("eprows", g["eprows2"][0:1, :], wait=(("dS", self.now("dS")),))
        self.tap("dtall", g["dtall"][:], wait=(("sc", self.now("sc")),))
        self.tap("d2all", g["d2all"][:], wait=(("sc", self.now("sc")),))
        self.tap("wcar", g["wcar"][:], wait=(("sc", self.now("sc")),))

    def attn_a_chunk(self, c):
        g = self.sb
        p = self.ps
        W = g["warena"]
        ms = getattr(self, "maxstep", 99)
        resc = g["res"][:, C * c:C * c + C]
        w0 = self.stats_done + self.wa_loaded + self.aconsts_done + self.hzw("wq_dW") if c == 0 else ()
        self.emit("scalar", lambda e: e.activation(
            g["xlnb"][:], resc, Act.Identity, bias=g["bcol"][:, c:c + 1],
            scale=g["rcol"][:, c:c + 1]), wait=w0 + self.hzw("xlnT_tn"), inc="sc")
        wtp = (("sc", self.now("sc")),) + self.free_wait("pbf")
        for i in range(NIC):
            self.emit("tensor", lambda e, i=i: e.transpose(
                p["pbf"][:, 128 * i:128 * i + 128], g["xlnb"][:, 128 * i:128 * i + 128],
                g["ident"][:]), wait=wtp if i == 0 else (), inc="tn")
        self.hzset("xlnT_tn", "tn")
        wt = (("tn", self.now("tn")),)
        x3 = g["xlnTc"][:].rearrange("p (a q) -> p a q", q=132)
        self.emit("vector", lambda e: e.tensor_copy(out=x3[:, :, 0:1], in_=x3[:, :, 128:129]),
                  wait=self.hzw("mix_gp") + self.hzw("xevac_ve") + (self.hzw("init_ms") if c == 0 else ()),
                  inc="ve")
        self.emit("vector", lambda e: e.tensor_copy(
            out=x3[:, :, 1:129], in_=p["pbf"][:].rearrange("p (a q) -> p a q", q=128)),
            wait=wt + (("ve", self.now("ve")),), inc="ve")
        self.set_free("pbf", "ve")
        self.hzset("xevac_ve", "ve")
        wx = (("ve", self.now("ve")),)
        xsh = x3[:, :, 0:128]
        xcur = x3[:, :, 1:129]
        Dm = g["sA"][:, 0:1024].rearrange("p (a q) -> p a q", q=128)
        tmp = g["sA"][:, 1024:2048].rearrange("p (a q) -> p a q", q=128)
        tmp2 = g["sB"][:, 0:1024].rearrange("p (a q) -> p a q", q=128)
        tmp3 = g["sB"][:, 1024:2048].rearrange("p (a q) -> p a q", q=128)
        self.emit("vector", lambda e: e.tensor_sub(out=Dm, in0=xcur, in1=xsh), wait=wx, inc="ve")
        wD = (("ve", self.now("ve")),)
        if ms < 2:
            return
        wreuse = self.hzw("proj_tn") + self.hzw("gated_ve")
        def mixop(eng, mixname, dst):
            w = wreuse + wD
            for ic in range(NIC):
                self.emit(eng, lambda e, ic=ic, mixname=mixname, dst=dst: e.scalar_tensor_tensor(
                    out=g[dst][:, 128 * ic:128 * ic + 128], in0=Dm[:, ic, :],
                    scalar=g[mixname][:, ic:ic + 1], in1=xsh[:, ic, :],
                    op0=Alu.mult, op1=Alu.add), wait=w if ic == 0 else (),
                    inc={"vector": "ve", "gpsimd": "gp"}[eng] if ic == NIC - 1 else None)
        mixop("vector", "mixk", "xkb")
        mixop("vector", "mixv", "xvb")
        mixop("vector", "mixr", "xrb")
        self.hzset("mix_gp", "ve")
        if ms < 3:
            return
        wmix = (("ve", self.now("ve")),)
        for wi, src, bk in ((0, "xkb", ("pb0", "pb1")), (1, "xvb", ("pb2", "pb3")),
                            (2, "xrb", ("pb4", "pb5"))):
            wfree = self.free_wait(bk[0]) + self.free_wait(bk[1])
            for ic in range(NIC):
                for oh in range(2):
                    self.emit("tensor", lambda e, wi=wi, src=src, bk=bk, ic=ic, oh=oh: e.matmul(
                        p[bk[oh]][:], g[src][:, 128 * ic:128 * ic + 128],
                        W[:, 8192 * wi + 1024 * ic + 512 * oh: 8192 * wi + 1024 * ic + 512 * oh + 512],
                        start=(ic == 0), stop=(ic == NIC - 1)),
                        wait=(wmix + wfree) if (ic == 0 and oh == 0) else (),
                        inc="tn" if (ic == NIC - 1 and oh == 1) else None)
        self.hzset("proj_tn", "tn")
        if ms < 4:
            return
        wproj = (("tn", self.now("tn")),)
        ek = g["sA"][:, 0:1024]
        self.emit("scalar", lambda e: e.activation(ek[:, 0:512], p["pb0"][:], Act.Exp), wait=wproj)
        self.emit("scalar", lambda e: e.activation(ek[:, 512:1024], p["pb1"][:], Act.Exp), inc="sc")
        self.set_free("pb0", "sc")
        self.set_free("pb1", "sc")
        SGF = getattr(self, "sgf", Act.Sigmoid)
        self.emit("scalar", lambda e: e.activation(g["sgr"][:, 0:512], p["pb4"][:], SGF),
                  wait=self.hzw("gated_ve"))
        self.emit("scalar", lambda e: e.activation(g["sgr"][:, 512:1024], p["pb5"][:], SGF), inc="sc")
        self.set_free("pb4", "sc")
        self.set_free("pb5", "sc")
        wek = (("sc", self.now("sc")),)
        self.emit("vector", lambda e: e.tensor_mul(out=g["pchunk"][:, 0:512], in0=ek[:, 0:512],
                                                   in1=p["pb2"][:]), wait=wek + self.hzw("gtr_tn"))
        self.emit("vector", lambda e: e.tensor_mul(out=g["pchunk"][:, 512:1024], in0=ek[:, 512:1024],
                                                   in1=p["pb3"][:]), inc="ve")
        self.set_free("pb2", "ve")
        self.set_free("pb3", "ve")
        self.emit("vector", lambda e: e.tensor_reduce(
            out=g["colscr"][:, 96:112], in_=ek.rearrange("p (h d) -> p h d", d=64), axis=AX, op=Alu.add),
            wait=self.hzw("dp_tn"), inc="ve")
        self.emit("vector", lambda e: e.tensor_copy(out=g["ekzb"][:], in_=g["colscr"][:, 96:112]),
                  wait=(("ve", self.now("ve")),), inc="ve")
        wP = (("ve", self.now("ve")),)
        if c == 0:
            self.tap("xlnTc_c0", g["xlnTc"][:], wait=(("ve", self.now("ve")),))
            self.tap("xkb_c0", g["xkb"][:], wait=(("ve", self.now("ve")),))
            self.tap("pch_c0", g["pchunk"][:], wait=wP)
            self.tap("ekzb_c0", g["ekzb"][:], wait=wP)
            self.tap("sgr_c0", g["sgr"][:], wait=(("sc", self.now("sc")),))
            self.tap("ek_c0", g["sA"][:, 0:1024], wait=(("sc", self.now("sc")),))
        if ms < 5:
            return
        noState = getattr(self, "noState", False)
        s5 = getattr(self, "s5", 9)
        wu = wP + self.free_wait("pb4") + self.free_wait("pb5")
        mms = []
        for h in range(H if not noState else 0):
            bku = "pb4" if h < 8 else "pb5"
            off = 64 * (h % 8)
            mms.append((lambda e, h=h, bku=bku, off=off: e.matmul(
                p[bku][0:1, off:off + 64], g["wcar"][:, h:h + 1], g["pchunk"][:, 64 * h:64 * h + 64],
                start=True, stop=True), wu if h == 0 else ()))
            mms.append((lambda e, h=h: e.matmul(
                p["pb4"][32:33, h:h + 1], g["wcar"][:, h:h + 1], g["ekzb"][:, h:h + 1],
                start=True, stop=True), ()))
        s = c % 2
        if c > 0 and not noState and s5 >= 2:
            wsn = (("dS", self.uscat_prev),) + (self.hzw("init_ms") if c == 1 else ())
            for h in range(H):
                bks = "pb4" if h < 8 else "pb5"
                off = 64 * (h % 8)
                mms.append((lambda e, h=h, bks=bks, off=off, c=c: e.matmul(
                    p[bks][64:65, off:off + 64], g["d2all"][0:16, 16 * h + c:16 * h + c + 1],
                    g["uallN"][0:16, 64 * h:64 * h + 64], start=True, stop=True),
                    wsn if h == 0 else ()))
                mms.append((lambda e, h=h, c=c: e.matmul(
                    p["pb4"][32:33, 16 + h:16 + h + 1], g["d2all"][0:16, 16 * h + c:16 * h + c + 1],
                    g["uallD"][0:16, h:h + 1], start=True, stop=True), ()))
        if mms:
            for i, (f, w) in enumerate(mms):
                self.emit("tensor", f, wait=w, inc="tn" if i == len(mms) - 1 else None)
        else:
            self.emit("tensor", lambda e: e.nop(), inc="tn")
        wUm = (("tn", self.now("tn")),)
        if not noState and s5 >= 3:
            self.emit("scalar", lambda e, s=s: e.activation(
                g["uscr"][:, 2048 * s:2048 * s + 512], p["pb4"][:], Act.Identity,
                bias=0.0, scale=1.0),
                wait=wUm + self.hzw("uscat_dS") + self.hzw("uscr%d_tn" % s))
            self.emit("scalar", lambda e, s=s: e.activation(
                g["uscr"][:, 2048 * s + 512:2048 * s + 1024], p["pb5"][:], Act.Identity,
                bias=0.0, scale=1.0), inc="sc")
        elif not noState and s5 < 3:
            self.emit("scalar", lambda e: e.nop(), wait=wUm, inc="sc")
        else:
            self.emit("vector", lambda e, s=s: e.memset(g["uscr"][:, 2048 * s:2048 * s + 2048], 0.0),
                      wait=wUm, inc="sc")
        wur = (("sc", self.now("sc")),)
        self.set_free("pb4", "sc")
        self.set_free("pb5", "sc")
        if not noState and s5 >= 4:
            self.emit("sync", lambda e, s=s: e.dma_start(out=g["uallN"][c:c + 1, :],
                                                         in_=g["uscr"][0:1, 2048 * s:2048 * s + 1024]),
                      wait=wur + (("dS", self.now("dS")),) + (self.hzw("init_ms") if c == 0 else ()),
                      inc="dS", n=16)
            self.emit("sync", lambda e, s=s: e.dma_start(out=g["uallD"][c:c + 1, :],
                                                         in_=g["uscr"][32:33, 2048 * s:2048 * s + 16]),
                      inc="dS", n=16)
            self.hzset("uscat_dS", "dS")
            self.uscat = self.now("dS")
        self.uscat_prev = getattr(self, "uscat", 0)
        wsr = (("sc", self.now("sc")),)
        if ms < 7:
            return
        wdp = wsr + wP + self.free_wait("pb2") + self.free_wait("pb3") + self.free_wait("pb6")
        mms = []
        for h in range(H):
            bkn = "pb2" if h < 8 else "pb3"
            off = 64 * (h % 8)
            noR1 = getattr(self, "noR1", False) or c == 0
            mms.append((lambda e, h=h, bkn=bkn, off=off, noR1=noR1: e.matmul(
                p[bkn][:, off:off + 64], g["dtall2"][:, 128 * h:128 * h + 128],
                g["pchunk"][:, 64 * h:64 * h + 64], start=True, stop=noR1),
                wdp if h == 0 else ()))
            if not noR1:
                mms.append((lambda e, h=h, bkn=bkn, off=off, s=s: e.matmul(
                    p[bkn][:, off:off + 64], g["eprows2"][64:65, 128 * h:128 * h + 128],
                    g["uscr"][64:65, 2048 * s + 512 * (h // 8) + 64 * (h % 8):
                              2048 * s + 512 * (h // 8) + 64 * (h % 8) + 64], start=False, stop=True), ()))
            mms.append((lambda e, h=h, s=s, noR1=noR1: e.matmul(
                p["pb6"][:, 32 * s + h:32 * s + h + 1], g["dtall"][:, 128 * h:128 * h + 128],
                g["ekzb"][:, h:h + 1], start=True, stop=noR1), ()))
            if not noR1:
                mms.append((lambda e, h=h, s=s: e.matmul(
                    p["pb6"][:, 32 * s + h:32 * s + h + 1], g["eprows2"][32:33, 128 * h:128 * h + 128],
                    g["uscr"][32:33, 2048 * s + 16 + h:2048 * s + 16 + h + 1], start=False, stop=True), ()))
        for i, (f, w) in enumerate(mms):
            self.emit("tensor", f, wait=w, inc="tn" if i == len(mms) - 1 else None)
        self.hzset("dp_tn", "tn")
        self.hzset("uscr%d_tn" % s, "tn")
        wdpd = (("tn", self.now("tn")),)
        if ms < 8:
            return
        cs = g["colscr"]
        self.emit("vector", lambda e: e.tensor_mul(out=cs[:, 32:48], in0=g["eucol"][:, 0:16],
                                                   in1=g["ekzb"][:, 0:16]), wait=wdpd, inc="ve")
        self.emit("vector", lambda e: e.tensor_scalar_max(out=cs[:, 32:48], in0=cs[:, 32:48], scalar1=1e-6),
                  wait=(("ve", self.now("ve")),), inc="ve")
        self.emit("vector", lambda e, s=s: e.tensor_tensor(out=cs[:, 48:64], in0=cs[:, 32:48],
                                                           in1=p["pb6"][:, 32 * s:32 * s + 16], op=Alu.add),
                  wait=(("ve", self.now("ve")),), inc="ve")
        self.set_free("pb6", "ve")
        self.emit("vector", lambda e: e.reciprocal(out=cs[:, 64:80], in_=cs[:, 48:64]),
                  wait=(("ve", self.now("ve")),), inc="ve")
        wrec2 = (("ve", self.now("ve")),)
        for h in range(H):
            bkn = "pb2" if h < 8 else "pb3"
            off = 64 * (h % 8)
            self.emit("vector", lambda e, h=h, bkn=bkn, off=off: e.scalar_tensor_tensor(
                out=g["pchunk"][:, 64 * h:64 * h + 64], in0=p[bkn][:, off:off + 64],
                scalar=cs[:, 64 + h:64 + h + 1], in1=g["sgr"][:, 64 * h:64 * h + 64],
                op0=Alu.mult, op1=Alu.mult), wait=wrec2 if h == 0 else (),
                inc="ve" if h == H - 1 else None)
        self.set_free("pb2", "ve")
        self.set_free("pb3", "ve")
        self.hzset("gated_ve", "ve")
        wg = (("ve", self.now("ve")),) + self.free_wait("pbf")
        if ms < 9:
            return
        for i in range(NIC):
            self.emit("tensor", lambda e, i=i: e.transpose(
                p["pbf"][:, 128 * i:128 * i + 128], g["pchunk"][:, 128 * i:128 * i + 128],
                g["ident"][:]), wait=wg if i == 0 else (), inc="tn")
        self.hzset("gtr_tn", "tn")
        self.emit("vector", lambda e: e.tensor_copy(out=g["gatedT"][:], in_=p["pbf"][:]),
                  wait=(("tn", self.now("tn")),) + self.hzw("wo_tn"), inc="ve")
        self.set_free("pbf", "ve")
        wgt = (("ve", self.now("ve")),) + self.free_wait("pb0") + self.free_wait("pb1")
        if ms < 10:
            return
        for ic in range(NIC):
            for oh in range(2):
                self.emit("tensor", lambda e, ic=ic, oh=oh: e.matmul(
                    p["pb0" if oh == 0 else "pb1"][:], g["gatedT"][:, 128 * ic:128 * ic + 128],
                    W[:, 8192 * 3 + 1024 * ic + 512 * oh: 8192 * 3 + 1024 * ic + 512 * oh + 512],
                    start=(ic == 0), stop=(ic == NIC - 1)),
                    wait=wgt if (ic == 0 and oh == 0) else (),
                    inc="tn" if (ic == NIC - 1 and oh == 1) else None)
        self.hzset("wo_tn", "tn")
        wwo = (("tn", self.now("tn")),)
        self.emit("vector", lambda e: e.tensor_add(out=resc[:, 0:512], in0=resc[:, 0:512],
                                                   in1=p["pb0"][:]), wait=wwo)
        self.emit("vector", lambda e: e.tensor_add(out=resc[:, 512:1024], in0=resc[:, 512:1024],
                                                   in1=p["pb1"][:]), inc="ve")
        self.set_free("pb0", "ve")
        self.set_free("pb1", "ve")

    def vchain(self, ops, wait0=()):
        for i, f in enumerate(ops):
            w = wait0 if i == 0 else (("ve", self.now("ve")),)
            self.emit("vector", f, wait=w, inc="ve")

    def wquant_one(self, dram_in, dram_out, ncols, nch, iswcol, batch=1):
        g = self.sb
        p = self.ps
        src = self.dram[dram_in].rearrange("(a q p) o -> p a q o", p=128, q=batch)
        dst = self.dram[dram_out].rearrange("(a q p) o -> p a q o", p=128, q=batch)
        ncols_eff = ncols * batch
        stage = g["sA"][:, 0:ncols_eff]
        st3 = stage.rearrange("p (q o) -> p q o", o=ncols)
        nh = (ncols_eff + 2047) // 2048
        for a in range(nch):
            self.emit("sync", lambda e, a=a, st3=st3: e.dma_start(out=st3, in_=src[:, a]),
                      wait=(("ve", self.now("ve")),), inc="dW", n=16)
            self.emit("vector", lambda e, a=a: e.tensor_reduce(
                out=g["fcol"][:, a:a + 1], in_=stage, axis=AX, op=Alu.add,
                apply_absolute_value=True), wait=(("dW", self.now("dW")),), inc="ve")
        self.vchain([
            lambda e: e.tensor_reduce(out=g["fcol"][:, 32:33], in_=g["fcol"][:, 0:nch],
                                      axis=AX, op=Alu.add)],
            wait0=(("ve", self.now("ve")),))
        self.emit("tensor", lambda e: e.matmul(p["pb6"][:, 0:1], g["onesq_f"][:],
                                               g["fcol"][:, 32:33], start=True, stop=True),
                  wait=(("ve", self.now("ve")),) + self.free_wait("pb6"), inc="tn")
        self.emit("scalar", lambda e: e.activation(g["fcol"][:, 33:34], p["pb6"][:, 0:1],
                                                   Act.Identity, bias=0.0, scale=1.0),
                  wait=(("tn", self.now("tn")),), inc="sc")
        self.set_free("pb6", "sc")
        nel = float(nch * 128 * ncols_eff)
        self.vchain([
            lambda e: e.tensor_scalar(out=iswcol, in0=g["fcol"][:, 33:34],
                                      scalar1=1.0 / nel, scalar2=1e-5,
                                      op0=Alu.mult, op1=Alu.max),
            lambda e: e.reciprocal(out=g["fcol"][:, 35:36], in_=iswcol)],
            wait0=(("sc", self.now("sc")),))
        wsw = (("ve", self.now("ve")),)
        for a in range(nch):
            self.emit("sync", lambda e, a=a, st3=st3: e.dma_start(out=st3, in_=src[:, a]),
                      wait=(("ve", self.now("ve")),), inc="dW", n=16)
            wd = (("dW", self.now("dW")),) + (wsw if a == 0 else ())
            for hh in range(nh):
                w = min(2048, ncols_eff - 2048 * hh)
                sl = slice(2048 * hh, 2048 * hh + w)
                self.emit("scalar", lambda e, sl=sl, w=w: e.activation(
                    g["sB"][:, 0:w], stage[:, sl], Act.Identity, bias=0.0,
                    scale=g["fcol"][:, 35:36]),
                    wait=wd + (("ve", self.now("ve")),), inc="sc")
                self.vchain([
                    lambda e, w=w: e.tensor_scalar(out=g["sC"][:, 0:w], in0=g["sB"][:, 0:w],
                                                   scalar1=C32, scalar2=-C32, op0=Alu.add, op1=Alu.add),
                    lambda e, w=w: e.tensor_scalar(out=g["sB"][:, 0:w], in0=g["sB"][:, 0:w],
                                                   scalar1=C52, scalar2=-C52, op0=Alu.add, op1=Alu.add),
                    lambda e, w=w: e.tensor_scalar_mul(out=g["sB"][:, 0:w], in0=g["sB"][:, 0:w],
                                                       scalar1=g["icol"][:]),
                    lambda e, w=w: e.scalar_tensor_tensor(out=g["sC"][:, 0:w], in0=g["sC"][:, 0:w],
                                                          scalar=g["jcol"][:], in1=g["sB"][:, 0:w],
                                                          op0=Alu.mult, op1=Alu.add),
                ], wait0=(("sc", self.now("sc")),))
                self.emit("vector", lambda e, sl=sl, w=w: e.tensor_scalar(
                    out=g["wq_bf"][:, sl], in0=g["sC"][:, 0:w], scalar1=-1.0, scalar2=1.0,
                    op0=Alu.max, op1=Alu.min),
                    wait=(("ve", self.now("ve")),) +
                         ((("dWs", self.now("dWs")),) if hh == 0 else ()), inc="ve")
            self.emit("sync", lambda e, a=a: e.dma_start(
                out=dst[:, a], in_=g["wq_bf"][:, 0:ncols_eff].rearrange(
                    "p (q o) -> p q o", o=ncols)),
                wait=(("ve", self.now("ve")),), inc="dWs", n=16)
        self.hz["wq_dW"] = ("dWs", self.now("dWs"))

    def phase_wquant(self):
        g = self.sb
        self.wquant_one("w1_t", "w1q_hbm", FFN_H, NIC, g["fcol"][:, 40:41])
        self.wquant_one("w2_t", "w2q_hbm", C, 8, g["fcol"][:, 41:42], batch=4)

    def phase_ffn(self, idx):
        g = self.sb
        p = self.ps
        W = g["warena"]
        wprev = (("tn", self.now("tn")), ("sc", self.now("sc")), ("ve", self.now("ve")))
        w1src = self.dram["w1q_hbm"].rearrange("(a p) o -> p a o", p=128)
        self.emit("sync", lambda e: e.dma_start(
            out=W[:].rearrange("p (a o) -> p a o", o=FFN_H), in_=w1src),
            wait=wprev + self.hzw("wq_dW"), inc="dW", n=16)
        self.emit("sync", lambda e: e.dma_start(out=g["sD"][0:1, 0:2048],
                                                in_=self.dram["b1_row"][0:1, 0:2048]), inc="dW", n=16)
        self.emit("sync", lambda e: e.dma_start(out=g["sD"][32:33, 0:2048],
                                                in_=self.dram["b1_row"][0:1, 2048:4096]), inc="dW", n=16)
        self.emit("sync", lambda e: e.dma_start(out=g["sD"][64:65, 0:1024],
                                                in_=self.dram["b2_row"]), inc="dW", n=16)
        wbias = (("dW", self.now("dW")),)
        self.emit("vector", lambda e: e.tensor_copy(out=g["brow"][0:1, :], in_=g["sD"][0:1, 0:2048]),
                  wait=wbias + wprev)
        self.emit("vector", lambda e: e.tensor_copy(out=g["brow"][32:33, :], in_=g["sD"][32:33, 0:2048]))
        self.emit("vector", lambda e: e.tensor_copy(out=g["brow"][64:65, 0:1024],
                                                    in_=g["sD"][64:65, 0:1024]), inc="ve")
        self.wffn = (("dW", self.now("dW")), ("ve", self.now("ve")))
        self.ln_stats(wait0=(("ve", self.now("ve")),))
        self.hz["ffn_dS"] = ("dS", self.now("dS"))
        for n2 in range(0, NCH, 2):
            self.ffn_pair(n2)
        self.tap("res_f%d" % idx, g["res"][:], wait=(("ve", self.now("ve")),))

    def ffn_quant_cols(self, n):
        g = self.sb
        fc = g["fcol"]
        resc = g["res"][:, C * n:C * n + C]
        self.emit("vector", lambda e, n=n: e.reduce_max(out=fc[:, 0:1], in_=resc, axis=AX),
                  wait=self.stats_done if n == 0 else (), inc="ve")
        self.emit("vector", lambda e, n=n: e.tensor_reduce(out=fc[:, 1:2], in_=resc, axis=AX,
                                                           op=Alu.min),
                  wait=(("ve", self.now("ve")),), inc="ve")
        self.vchain([
            lambda e, n=n: e.reciprocal(out=fc[:, 2:3], in_=g["rcol"][:, n:n + 1]),
            lambda e: e.tensor_tensor(out=fc[:, 2:3], in0=fc[:, 2:3], in1=fc[:, 2:3], op=Alu.mult),
            lambda e: e.tensor_scalar(out=fc[:, 2:3], in0=fc[:, 2:3], scalar1=-1e-5,
                                      scalar2=float(1 + 1e-6), op0=Alu.add, op1=Alu.mult),
            lambda e: e.tensor_scalar_add(out=fc[:, 2:3], in0=fc[:, 2:3], scalar1=1e-11),
        ], wait0=(("ve", self.now("ve")),))
        self.emit("scalar", lambda e: e.activation(fc[:, 3:4], fc[:, 2:3], Act.Sqrt),
                  wait=(("ve", self.now("ve")),), inc="sc")
        self.vchain([
            lambda e: e.reciprocal(out=fc[:, 4:5], in_=fc[:, 3:4]),
            lambda e, n=n: e.tensor_tensor(out=fc[:, 5:6], in0=fc[:, 0:1],
                                           in1=g["mcol"][:, n:n + 1], op=Alu.subtract),
            lambda e, n=n: e.tensor_tensor(out=fc[:, 6:7], in0=g["mcol"][:, n:n + 1],
                                           in1=fc[:, 1:2], op=Alu.subtract),
            lambda e: e.tensor_tensor(out=fc[:, 5:6], in0=fc[:, 5:6], in1=fc[:, 6:7], op=Alu.max),
            lambda e: e.tensor_tensor(out=fc[:, 5:6], in0=fc[:, 5:6], in1=fc[:, 4:5], op=Alu.mult),
            lambda e: e.tensor_scalar_max(out=fc[:, 5:6], in0=fc[:, 5:6], scalar1=1e-5),
            lambda e: e.reciprocal(out=fc[:, 6:7], in_=fc[:, 5:6]),
            lambda e: e.tensor_scalar_mul(out=fc[:, 6:7], in0=fc[:, 6:7], scalar1=127.0),
            lambda e: e.tensor_tensor(out=fc[:, 8:9], in0=fc[:, 4:5], in1=fc[:, 6:7], op=Alu.mult),
            lambda e, n=n: e.scalar_tensor_tensor(out=fc[:, 9:10], in0=g["mcol"][:, n:n + 1],
                                                  scalar=-1.0, in1=fc[:, 8:9],
                                                  op0=Alu.mult, op1=Alu.mult),
            lambda e: e.tensor_tensor(out=fc[:, 7:8], in0=fc[:, 5:6], in1=g["fcol"][:, 40:41],
                                      op=Alu.mult),
            lambda e: e.tensor_scalar_mul(out=fc[:, 7:8], in0=fc[:, 7:8], scalar1=1.0 / 127.0),
            lambda e: e.tensor_tensor(out=fc[:, 10:11], in0=fc[:, 7:8], in1=g["jcol"][:], op=Alu.mult),
            lambda e: e.tensor_tensor(out=fc[:, 11:12], in0=fc[:, 7:8], in1=g["icol"][:], op=Alu.mult),
        ], wait0=(("sc", self.now("sc")),))

    def ffn_tile_h(self, n, slot):
        g = self.sb
        p = self.ps
        fc = g["fcol"]
        W = g["warena"]
        resc = g["res"][:, C * n:C * n + C]
        self.ffn_quant_cols(n)
        wcols = (("ve", self.now("ve")),)
        self.emit("scalar", lambda e: e.activation(g["sC"][:, 0:1024], resc, Act.Identity,
                                                   bias=fc[:, 9:10], scale=fc[:, 8:9]),
                  wait=wcols, inc="sc")
        self.vchain([
            lambda e: e.tensor_scalar(out=g["sC"][:, 1024:2048], in0=g["sC"][:, 0:1024],
                                      scalar1=C32, scalar2=-C32, op0=Alu.add, op1=Alu.add),
            lambda e: e.tensor_scalar(out=g["sB"][:, 0:1024], in0=g["sC"][:, 0:1024],
                                      scalar1=C52, scalar2=-C52, op0=Alu.add, op1=Alu.add),
            lambda e: e.tensor_scalar_mul(out=g["sB"][:, 0:1024], in0=g["sB"][:, 0:1024],
                                          scalar1=fc[:, 11:12]),
            lambda e: e.scalar_tensor_tensor(out=g["xqb"][:], in0=g["sC"][:, 1024:2048],
                                             scalar=fc[:, 10:11], in1=g["sB"][:, 0:1024],
                                             op0=Alu.mult, op1=Alu.add),
        ], wait0=(("sc", self.now("sc")),))
        wq = (("ve", self.now("ve")),) + self.free_wait("pbf")
        for i in range(NIC):
            self.emit("tensor", lambda e, i=i: e.transpose(
                p["pbf"][:, 128 * i:128 * i + 128], g["xqb"][:, 128 * i:128 * i + 128],
                g["ident"][:]), wait=wq if i == 0 else (), inc="tn" if i == NIC - 1 else None)
        self.emit("vector", lambda e: e.tensor_copy(out=g["xqT"][:], in_=p["pbf"][:]),
                  wait=(("tn", self.now("tn")),) + self.hzw("xqT_tn"), inc="ve")
        self.set_free("pbf", "ve")
        wxt = (("ve", self.now("ve")),)
        for ho in range(2):
            wfree = tuple(self.free_wait("pb%d" % b)[0] for b in range(4))
            mms = []
            for ic in range(NIC):
                for ot in range(4):
                    o0 = 2048 * ho + 512 * ot
                    mms.append((lambda e, ic=ic, ot=ot, o0=o0: e.matmul(
                        p["pb%d" % ot][:], g["xqT"][:, 128 * ic:128 * ic + 128],
                        W[:, 4096 * ic + o0:4096 * ic + o0 + 512],
                        start=(ic == 0), stop=False),
                        (wxt + wfree + self.wffn) if (ic == 0 and ot == 0) else ()))
            bb = 0 if ho == 0 else 32
            for ot in range(4):
                mms.append((lambda e, ot=ot, bb=bb: e.matmul(
                    p["pb%d" % ot][:], g["onesq_b"][bb:bb + 1, :],
                    g["brow"][bb:bb + 1, 512 * ot:512 * ot + 512],
                    start=False, stop=True), ()))
            for i, (f, w) in enumerate(mms):
                self.emit("tensor", f, wait=w, inc="tn" if i == len(mms) - 1 else None)
            if ho == 1:
                self.hzset("xqT_tn", "tn")
            wmm = (("tn", self.now("tn")),)
            for ot in range(4):
                self.emit("scalar", lambda e, ho=ho, ot=ot: e.activation(
                    g["sA"][:, 2048 * ho + 512 * ot:2048 * ho + 512 * ot + 512],
                    p["pb%d" % ot][:], Act.Gelu),
                    wait=wmm if ot == 0 else (), inc="sc" if ot == 3 else None)
            for ot in range(4):
                self.set_free("pb%d" % ot, "sc")
        wh = (("sc", self.now("sc")),)
        self.emit("vector", lambda e: e.reduce_max(out=fc[:, 18:19], in_=g["sA"][:, 0:2048],
                                                   axis=AX, apply_absolute_value=True), wait=wh)
        self.emit("vector", lambda e: e.reduce_max(out=fc[:, 19:20], in_=g["sA"][:, 2048:4096],
                                                   axis=AX, apply_absolute_value=True), inc="ve")
        self.emit("scalar", lambda e: e.activation(g["sC"][:, 0:2048], g["sA"][:, 0:2048],
                                                   Act.Square), wait=wh, inc="sc")
        self.emit("vector", lambda e: e.tensor_reduce(out=fc[:, 16:17], in_=g["sC"][:, 0:2048],
                                                      axis=AX, op=Alu.add),
                  wait=(("sc", self.now("sc")),), inc="ve")
        self.emit("scalar", lambda e: e.activation(g["sC"][:, 0:2048], g["sA"][:, 2048:4096],
                                                   Act.Square), wait=(("ve", self.now("ve")),), inc="sc")
        self.emit("vector", lambda e: e.tensor_reduce(out=fc[:, 17:18], in_=g["sC"][:, 0:2048],
                                                      axis=AX, op=Alu.add),
                  wait=(("sc", self.now("sc")),), inc="ve")
        self.vchain([
            lambda e: e.tensor_tensor(out=fc[:, 16:17], in0=fc[:, 16:17], in1=fc[:, 17:18], op=Alu.add),
            lambda e: e.tensor_scalar(out=fc[:, 20:21], in0=fc[:, 16:17], scalar1=1.0 / FFN_H,
                                      scalar2=1e-6, op0=Alu.mult, op1=Alu.add),
        ], wait0=(("ve", self.now("ve")),))
        self.emit("scalar", lambda e: e.activation(fc[:, 21:22], fc[:, 20:21], Act.Sqrt),
                  wait=(("ve", self.now("ve")),), inc="sc")
        self.vchain([
            lambda e: e.reciprocal(out=fc[:, 22:23], in_=fc[:, 21:22]),
            lambda e: e.tensor_tensor(out=fc[:, 18:19], in0=fc[:, 18:19], in1=fc[:, 19:20], op=Alu.max),
            lambda e: e.tensor_tensor(out=fc[:, 18:19], in0=fc[:, 18:19], in1=fc[:, 22:23], op=Alu.mult),
            lambda e: e.tensor_scalar_max(out=fc[:, 18:19], in0=fc[:, 18:19], scalar1=1e-5),
            lambda e: e.reciprocal(out=fc[:, 23:24], in_=fc[:, 18:19]),
            lambda e: e.tensor_scalar_mul(out=fc[:, 23:24], in0=fc[:, 23:24], scalar1=127.0),
            lambda e: e.tensor_tensor(out=fc[:, 24:25], in0=fc[:, 22:23], in1=fc[:, 23:24], op=Alu.mult),
            lambda e: e.tensor_tensor(out=fc[:, 25:26], in0=fc[:, 18:19], in1=g["fcol"][:, 41:42],
                                      op=Alu.mult),
            lambda e: e.tensor_scalar_mul(out=fc[:, 25:26], in0=fc[:, 25:26], scalar1=1.0 / 127.0),
            lambda e: e.tensor_tensor(out=fc[:, 26:27], in0=fc[:, 25:26], in1=g["jcol"][:], op=Alu.mult),
            lambda e: e.tensor_tensor(out=fc[:, 27:28], in0=fc[:, 25:26], in1=g["icol"][:], op=Alu.mult),
        ], wait0=(("sc", self.now("sc")),))
        wc2 = (("ve", self.now("ve")),)
        hqT = g["hqT%d" % slot]
        for hh in range(2):
            sl = slice(2048 * hh, 2048 * hh + 2048)
            self.emit("scalar", lambda e, sl=sl: e.activation(g["sB"][:, 0:2048], g["sA"][:, sl],
                                                              Act.Identity, bias=0.0,
                                                              scale=fc[:, 24:25]),
                      wait=(wc2 if hh == 0 else ()) + (("ve", self.now("ve")),), inc="sc")
            whb = (("tn", self.now("tn")),) if hh == 1 else ()
            self.vchain([
                lambda e: e.tensor_scalar(out=g["sC"][:, 0:2048], in0=g["sB"][:, 0:2048],
                                          scalar1=C32, scalar2=-C32, op0=Alu.add, op1=Alu.add),
                lambda e, sl=sl: e.tensor_scalar(out=g["sA"][:, sl], in0=g["sB"][:, 0:2048],
                                                 scalar1=C52, scalar2=-C52, op0=Alu.add, op1=Alu.add),
                lambda e, sl=sl: e.tensor_scalar_mul(out=g["sA"][:, sl], in0=g["sA"][:, sl],
                                                     scalar1=fc[:, 27:28]),
                lambda e, sl=sl: e.scalar_tensor_tensor(
                    out=g["hqb"][:], in0=g["sC"][:, 0:2048],
                    scalar=fc[:, 26:27], in1=g["sA"][:, sl], op0=Alu.mult, op1=Alu.add),
            ], wait0=(("sc", self.now("sc")),) + whb)
            for r in range(2):
                wtr = (("ve", self.now("ve")),) + self.free_wait("pbf")
                for i in range(8):
                    self.emit("tensor", lambda e, r=r, i=i: e.transpose(
                        p["pbf"][:, 128 * i:128 * i + 128],
                        g["hqb"][:, 1024 * r + 128 * i:1024 * r + 128 * i + 128],
                        g["ident"][:]), wait=wtr if i == 0 else (), inc="tn" if i == 7 else None)
                self.emit("vector", lambda e, r=r, hh=hh, hqT=hqT: e.tensor_copy(
                    out=hqT[:, 2048 * hh + 1024 * r:2048 * hh + 1024 * r + 1024], in_=p["pbf"][:]),
                    wait=(("tn", self.now("tn")),) +
                         ((self.hzw("mm2_tn") + self.hzw("ffn_dS"))
                          if (hh == 0 and r == 0) else ()), inc="ve")
                self.set_free("pbf", "ve")
        self.hz["hqT%d_ve" % slot] = ("ve", self.now("ve"))

    def ffn_pair(self, n2):
        for k in range(2):
            self.ffn_tile_h(n2 + k, k)
        self.ffn_mm2_pair(n2)

    def ffn_mm2_pair(self, n2):
        g = self.sb
        p = self.ps
        w2src = self.dram["w2q_hbm"].rearrange("(a p) o -> p a o", p=128)
        wh = (self.hz["hqT0_ve"], self.hz["hqT1_ve"])
        wfree = tuple(self.free_wait("pb%d" % b)[0] for b in range(4))
        buf_tok = [self.hz.get("mm2_tn", ("tn", 0))] * 2
        for hc in range(32):
            buf = g["wbuf"][:, 1024 * (hc % 2):1024 * (hc % 2) + 1024]
            qn, sm = ("sync", "dW2") if hc % 2 == 0 else ("gpsimd", "dW2g")
            self.emit(qn, lambda e, buf=buf, hc=hc: e.dma_start(out=buf, in_=w2src[:, hc, :]),
                      wait=(buf_tok[hc % 2],) + ((("%s" % sm, self.now(sm)),) if hc >= 2
                                                 else self.hzw("wq_dW")),
                      inc=sm, n=16)
            wd = ((sm, self.now(sm)),)
            mms = []
            for mt in range(2):
                hqT = g["hqT%d" % mt]
                for oh in range(2):
                    bk = "pb%d" % (2 * mt + oh)
                    mms.append((lambda e, hc=hc, bk=bk, oh=oh, buf=buf, hqT=hqT: e.matmul(
                        p[bk][:], hqT[:, 128 * hc:128 * hc + 128],
                        buf[:, 512 * oh:512 * oh + 512],
                        start=(hc == 0), stop=False),
                        (wd + ((wh + wfree) if hc == 0 else ())) if (mt == 0 and oh == 0) else ()))
            for i, (f, w) in enumerate(mms):
                self.emit("tensor", f, wait=w, inc="tn" if i == len(mms) - 1 else None)
            buf_tok[hc % 2] = ("tn", self.now("tn"))
        mms = []
        for mt in range(2):
            for oh in range(2):
                bk = "pb%d" % (2 * mt + oh)
                mms.append((lambda e, bk=bk, oh=oh: e.matmul(
                    p[bk][:], g["onesq_b"][64:65, :], g["brow"][64:65, 512 * oh:512 * oh + 512],
                    start=False, stop=True), ()))
        for i, (f, w) in enumerate(mms):
            self.emit("tensor", f, wait=w, inc="tn" if i == len(mms) - 1 else None)
        self.hzset("mm2_tn", "tn")
        wmm2 = (("tn", self.now("tn")),)
        for mt in range(2):
            resc = g["res"][:, C * (n2 + mt):C * (n2 + mt) + C]
            for oh in range(2):
                self.emit("vector", lambda e, oh=oh, mt=mt, resc=resc: e.tensor_add(
                    out=resc[:, 512 * oh:512 * oh + 512], in0=resc[:, 512 * oh:512 * oh + 512],
                    in1=p["pb%d" % (2 * mt + oh)][:]),
                    wait=wmm2 if (mt == 0 and oh == 0) else (),
                    inc="ve" if (mt == 1 and oh == 1) else None)
        for b in range(4):
            self.set_free("pb%d" % b, "ve")

    def phase_attn_b(self):
        g = self.sb
        SCB = float(64.0 ** -0.25)
        wprev = (("tn", self.now("tn")), ("sc", self.now("sc")), ("ve", self.now("ve")))
        self.hz.pop("wl_0", None)
        self.hz.pop("wl_1", None)
        for wi, (nm, s) in enumerate([("wq_t", SCB), ("wkb_t", SCB),
                                      ("wvb_t", None), ("wbo_t", None)]):
            self.load_warena_quadrant(nm, wi, scale=s)
        self.wb_loaded = (("ve", self.now("ve")),)
        self.emit("sync", lambda e: e.dma_start(out=g["sD"][0:1, 0:1024], in_=self.dram["bq_row"]),
                  wait=wprev + (("ve", self.now("ve")),), inc="dW", n=16)
        self.emit("sync", lambda e: e.dma_start(out=g["sD"][0:1, 1024:2048],
                                                in_=self.dram["bo_row"]), inc="dW", n=16)
        self.emit("sync", lambda e: e.dma_start(out=g["sD"][32:33, 0:1024],
                                                in_=self.dram["bk_row"]), inc="dW", n=16)
        self.emit("sync", lambda e: e.dma_start(out=g["sD"][64:65, 0:1024],
                                                in_=self.dram["bv_row"]), inc="dW", n=16)
        wb = (("dW", self.now("dW")),)
        self.emit("vector", lambda e: e.tensor_scalar_mul(out=g["brow_b"][0:1, 0:1024],
                                                          in0=g["sD"][0:1, 0:1024], scalar1=SCB),
                  wait=wb + wprev)
        self.emit("vector", lambda e: e.tensor_copy(out=g["brow_b"][0:1, 1024:2048],
                                                    in_=g["sD"][0:1, 1024:2048]))
        self.emit("vector", lambda e: e.tensor_scalar_mul(out=g["brow_b"][32:33, 0:1024],
                                                          in0=g["sD"][32:33, 0:1024], scalar1=SCB))
        self.emit("vector", lambda e: e.tensor_copy(out=g["brow_b"][64:65, 0:1024],
                                                    in_=g["sD"][64:65, 0:1024]), inc="ve")
        self.emit("vector", lambda e: e.memset(g["projT2"][:], 0.0), wait=wprev, inc="ve")
        self.emit("sync", lambda e: e.dma_start(out=g["wscr"][0:64, 0:128], in_=self.dram["proj_t"]),
                  wait=wprev + (("ve", self.now("ve")),), inc="dC", n=16)
        self.emit("sync", lambda e: e.dma_start(out=g["wscr"][64:128, 0:128],
                                                in_=self.dram["proj_t"]), inc="dC", n=16)
        wpj = (("dC", self.now("dC")),)
        self.emit("vector", lambda e: e.tensor_copy(out=g["projT2"][0:64, 0:128],
                                                    in_=g["wscr"][0:64, 0:128]),
                  wait=wpj + (("ve", self.now("ve")),), inc="ve")
        self.emit("vector", lambda e: e.tensor_copy(out=g["projT2"][64:128, 128:256],
                                                    in_=g["wscr"][64:128, 0:128]), inc="ve")
        self.emit("vector", lambda e: e.memset(g["vc65_b"][:], 1.0), wait=wprev, inc="ve")
        self.emit("vector", lambda e: e.memset(g["sA"][:, 0:1040], 0.0), inc="ve")
        self.emit("vector", lambda e: e.memset(g["Sb16_b"][:], 0.0), inc="ve")
        self.hzset("sb16_ve", "ve")
        self.hzset("projT2_ve", "ve")
        self.ln_stats(wait0=(("ve", self.now("ve")),))
        for t in (["qb_tn", "abT_tn", "vc65_ve", "qfT_ve", "kfT_ve", "kfb_ve"] +
                  ["ndf%d" % i for i in range(8)] + ["kvf%d" % i for i in range(8)] +
                  ["ndb%d" % i for i in range(2)] +
                  ["atf%d" % i for i in range(2)] + ["atmf%d" % i for i in range(2)]):
            self.hz.pop(t, None)
        for c in range(getattr(self, "b_nchunks", NCH)):
            self.attn_b_chunk(c)
        self.tap("res_b", g["res"][:], wait=(("ve", self.now("ve")),))

    def attn_b_side(self, side):
        g = self.sb
        p = self.ps
        xb = g["qb_b" if side == 0 else "kb_b"]
        xT = g["qbT_b" if side == 0 else "kbT_b"]
        fb = g["qfb_b" if side == 0 else "kfb_b"]
        bc0 = 0 if side == 0 else 16
        wtr = (("sc", self.now("sc")),) + self.free_wait("pbf")
        for i in range(NIC):
            self.emit("tensor", lambda e, i=i, xb=xb: e.transpose(
                p["pbf"][:, 128 * i:128 * i + 128], xb[:, 128 * i:128 * i + 128],
                g["ident"][:]), wait=wtr if i == 0 else (), inc="tn" if i == NIC - 1 else None)
        if side == 0:
            self.hzset("qb_tn", "tn")
        self.emit("vector", lambda e, xT=xT: e.tensor_copy(out=xT[:], in_=p["pbf"][:]),
                  wait=(("tn", self.now("tn")),) + self.hzw("abT_tn"), inc="ve")
        self.set_free("pbf", "ve")
        if getattr(self, "b_substop", None) == "xT":
            return
        wut0 = (("ve", self.now("ve")),) + self.hzw("projT2_ve")
        for j in range(8):
            ub = p["pb%d" % (5 + j % 2)]
            self.emit("tensor", lambda e, j=j, ub=ub, xT=xT: e.matmul(
                ub[:, 0:256], xT[:, 128 * j:128 * j + 128],
                g["projT2"][:], start=True, stop=True),
                wait=(wut0 if j == 0 else ()) + self.hzw("u_f%d" % (j % 2)), inc="tn")
            wexp = (("tn", self.now("tn")),) + ((("ve", self.bcol_ve),) if j == 0 else ())
            if getattr(self, "b_substop", None) == "umm":
                self.hz["u_f%d" % (j % 2)] = ("tn", self.now("tn"))
                continue
            for k in range(2):
                self.emit("scalar", lambda e, j=j, k=k, ub=ub, fb=fb, bc0=bc0: e.activation(
                    fb[:, 256 * j + 128 * k:256 * j + 128 * k + 128],
                    ub[:, 128 * k:128 * k + 256 - 128], Act.Exp,
                    bias=g["fcol"][:, bc0 + 2 * j + k:bc0 + 2 * j + k + 1], scale=1.0),
                    wait=wexp if k == 0 else (), inc="sc" if k == 1 else None)
            self.hz["u_f%d" % (j % 2)] = ("sc", self.now("sc"))
        if getattr(self, "b_substop", None) in ("umm", "exp"):
            return
        self.emit("vector", lambda e, fb=fb: e.tensor_scalar_add(out=fb[:], in0=fb[:],
                                                                 scalar1=1e-6),
                  wait=(("sc", self.now("sc")),), inc="ve")
        if side == 1:
            self.hzset("kfb_ve", "ve")

    def attn_b_side_T(self, side):
        g = self.sb
        p = self.ps
        fb = g["qfb_b" if side == 0 else "kfb_b"]
        fT = g["qfT_b" if side == 0 else "kfT_b"]
        for r in range(2):
            wtr = (("ve", self.now("ve")),) + self.free_wait("pbf")
            for i in range(NIC):
                self.emit("tensor", lambda e, r=r, i=i, fb=fb: e.transpose(
                    p["pbf"][:, 128 * i:128 * i + 128],
                    fb[:, 1024 * r + 128 * i:1024 * r + 128 * i + 128],
                    g["ident"][:]), wait=wtr if i == 0 else (), inc="tn" if i == NIC - 1 else None)
            self.emit("vector", lambda e, r=r, fT=fT: e.tensor_copy(
                out=fT[:, 1024 * r:1024 * r + 1024], in_=p["pbf"][:]),
                wait=(("tn", self.now("tn")),), inc="ve")
            self.set_free("pbf", "ve")
        self.hzset("qfT_ve" if side == 0 else "kfT_ve", "ve")

    def attn_b_head_pre(self, h, c):
        g = self.sb
        p = self.ps
        nd = p["pb%d" % (2 + h % 2)][:, 128 * ((h // 2) % 4):128 * ((h // 2) % 4) + 65]
        w0 = (self.hzw("qfT_ve") + self.hzw("sb16_ve")) if h == 0 else ()
        self.emit("tensor", lambda e, h=h, nd=nd: e.matmul(
            nd, g["qfT_b"][:, 128 * h:128 * h + 128], g["Sb16_b"][:, 65 * h:65 * h + 65],
            start=True, stop=False),
            wait=w0 + self.hzw("ndf%d" % (h % 8)) + self.hzw("ndb%d" % (h % 2)))
        pa = p["pb%d" % (h % 2)][:, 0:128]
        self.emit("tensor", lambda e, h=h, pa=pa: e.matmul(
            pa, g["kfT_b"][:, 128 * h:128 * h + 128], g["qfT_b"][:, 128 * h:128 * h + 128],
            start=True, stop=True),
            wait=(self.hzw("kfT_ve") if h == 0 else ()) + self.hzw("atf%d" % (h % 2)), inc="tn")
        self.hz["at_tn%d" % (h % 2)] = ("tn", self.now("tn"))

    def attn_b_head_mask(self, h):
        g = self.sb
        p = self.ps
        pa = p["pb%d" % (h % 2)][:, 0:128]
        sl = g["sB"][:, 128 * (h % 2):128 * (h % 2) + 128]
        self.emit("vector", lambda e, pa=pa, sl=sl: e.tensor_mul(out=sl, in0=pa,
                                                                 in1=g["mask_ec"][:]),
                  wait=(self.hz["at_tn%d" % (h % 2)],), inc="ve")
        self.hz["atf%d" % (h % 2)] = ("ve", self.now("ve"))
        self.emit("vector", lambda e, h=h, sl=sl: e.tensor_copy(
            out=g["atm_b"][h % 2][:], in_=sl),
            wait=self.hzw("atmf%d" % (h % 2)) + (("ve", self.now("ve")),), inc="ve")
        self.hz["atm_ve%d" % (h % 2)] = ("ve", self.now("ve"))

    def attn_b_head_post(self, h, c):
        g = self.sb
        p = self.ps
        nd = p["pb%d" % (2 + h % 2)][:, 128 * ((h // 2) % 4):128 * ((h // 2) % 4) + 65]
        kv = p["pb%d" % (4 + (h % 8) // 4)][:, 128 * (h % 4):128 * (h % 4) + 65]
        self.emit("tensor", lambda e, h=h, nd=nd: e.matmul(
            nd, g["atm_b"][h % 2][:], g["vc65_b"][:, 65 * h:65 * h + 65],
            start=False, stop=True),
            wait=(self.hz["atm_ve%d" % (h % 2)],) +
                 (self.hzw("vc65_ve") if h == 0 else ()), inc="tn")
        self.hz["atmf%d" % (h % 2)] = ("tn", self.now("tn"))
        ndtn = ("tn", self.now("tn"))
        self.emit("tensor", lambda e, h=h, kv=kv: e.matmul(
            kv, g["kfb_b"][:, 128 * h:128 * h + 128], g["vc65_b"][:, 65 * h:65 * h + 65],
            start=True, stop=True),
            wait=(self.hzw("kfb_ve") if h == 0 else ()) + self.hzw("kvf%d" % (h % 8)), inc="tn")
        self.emit("vector", lambda e, h=h, kv=kv: e.tensor_add(
            out=g["sA"][:, 65 * h:65 * h + 65], in0=g["sA"][:, 65 * h:65 * h + 65], in1=kv),
            wait=(("tn", self.now("tn")),), inc="ve")
        self.hz["kvf%d" % (h % 8)] = ("ve", self.now("ve"))
        dc, rc = 42 + (h % 2), 44 + (h % 2)
        self.emit("vector", lambda e, nd=nd, dc=dc: e.tensor_scalar_add(
            out=g["fcol"][:, dc:dc + 1], in0=nd[:, 64:65], scalar1=1e-6), wait=(ndtn,), inc="ve")
        self.vchain([
            lambda e, dc=dc, rc=rc: e.reciprocal(out=g["fcol"][:, rc:rc + 1],
                                                 in_=g["fcol"][:, dc:dc + 1]),
            lambda e, h=h, nd=nd, rc=rc: e.tensor_scalar_mul(
                out=g["ob_b"][:, 64 * h:64 * h + 64], in0=nd[:, 0:64],
                scalar1=g["fcol"][:, rc:rc + 1]),
        ], wait0=(("ve", self.now("ve")),))
        self.hz["ndf%d" % (h % 8)] = ("ve", self.now("ve"))
        self.hz["ndb%d" % (h % 2)] = ("ve", self.now("ve"))

    def attn_b_chunk(self, c):
        g = self.sb
        p = self.ps
        LNM = float(np.log(np.sqrt(np.float32(128.0))))
        resc = g["res"][:, C * c:C * c + C]
        w0 = (self.stats_done + self.wb_loaded + (("tn", self.now("tn")),)) if c == 0 else ()
        self.emit("scalar", lambda e, c=c: e.activation(
            g["xlnb_b"][:], resc, Act.Identity, bias=g["bcol"][:, c:c + 1],
            scale=g["rcol"][:, c:c + 1]), wait=w0 + self.hzw("qb_tn"), inc="sc")
        wtp = (("sc", self.now("sc")),) + self.free_wait("pbf")
        for i in range(NIC):
            self.emit("tensor", lambda e, i=i: e.transpose(
                p["pbf"][:, 128 * i:128 * i + 128], g["xlnb_b"][:, 128 * i:128 * i + 128],
                g["ident"][:]), wait=wtp if i == 0 else (), inc="tn" if i == NIC - 1 else None)
        self.emit("vector", lambda e: e.tensor_copy(out=g["xnT_b"][:], in_=p["pbf"][:]),
                  wait=(("tn", self.now("tn")),) + self.hzw("abT_tn"), inc="ve")
        self.set_free("pbf", "ve")
        if getattr(self, "b_stop", None) == "xn":
            return
        first = True
        for wi in range(3):
            br = (0, 32, 64)[wi]
            for oh in range(2):
                bank = p["pb%d" % (2 * wi + oh)]
                for ic in range(NIC):
                    self.emit("tensor", lambda e, wi=wi, oh=oh, ic=ic, bank=bank: e.matmul(
                        bank[:], g["xnT_b"][:, 128 * ic:128 * ic + 128],
                        g["warena"][:, 8192 * wi + 1024 * ic + 512 * oh:
                                    8192 * wi + 1024 * ic + 512 * oh + 512],
                        start=(ic == 0), stop=False),
                        wait=(("ve", self.now("ve")),) if first else ())
                    first = False
                self.emit("tensor", lambda e, br=br, oh=oh, bank=bank: e.matmul(
                    bank[:], g["onesq_b"][br:br + 1, :],
                    g["brow_b"][br:br + 1, 512 * oh:512 * oh + 512],
                    start=False, stop=True), inc="tn")
        wmm = (("tn", self.now("tn")),)
        if getattr(self, "b_stop", None) == "qkv":
            return
        for i, (bank, dst) in enumerate([(p["pb0"], g["qb_b"][:, 0:512]),
                                         (p["pb1"], g["qb_b"][:, 512:1024]),
                                         (p["pb2"], g["kb_b"][:, 0:512]),
                                         (p["pb3"], g["kb_b"][:, 512:1024]),
                                         (p["pb4"], g["ob_b"][:, 0:512]),
                                         (p["pb5"], g["ob_b"][:, 512:1024])]):
            self.emit("scalar", lambda e, bank=bank, dst=dst: e.activation(
                dst, bank[:], Act.Identity, bias=0.0, scale=1.0),
                wait=wmm if i == 0 else (), inc="sc")
        self.emit("vector", lambda e: e.tensor_copy(
            out=g["vc65_b"][:].rearrange("p (gp d) -> p gp d", d=65)[:, :, 0:64],
            in_=g["ob_b"][:].rearrange("p (gp d) -> p gp d", d=64)),
            wait=(("sc", self.now("sc")),), inc="ve")
        self.hzset("vc65_ve", "ve")
        for side, xb in ((0, g["qb_b"]), (1, g["kb_b"])):
            self.emit("scalar", lambda e, xb=xb: e.activation(g["sB"][:, 0:1024], xb[:],
                                                              Act.Square),
                      wait=(("sc", self.now("sc")), ("ve", self.now("ve"))), inc="sc")
            self.emit("vector", lambda e, side=side: e.tensor_reduce(
                out=g["fcol"][:, 16 * side:16 * side + 16],
                in_=g["sB"][:, 0:1024].rearrange("p (gp d) -> p gp d", d=64),
                axis=AX, op=Alu.add), wait=(("sc", self.now("sc")),), inc="ve")
            self.emit("vector", lambda e, side=side, LNM=LNM: e.tensor_scalar(
                out=g["fcol"][:, 16 * side:16 * side + 16],
                in0=g["fcol"][:, 16 * side:16 * side + 16],
                scalar1=-0.5, scalar2=-LNM, op0=Alu.mult, op1=Alu.add),
                wait=(("ve", self.now("ve")),), inc="ve")
        self.bcol_ve = self.now("ve")
        if getattr(self, "b_stop", None) == "evac":
            return
        if c == 0:
            self.tap("b_qb", g["qb_b"][:], wait=(("sc", self.now("sc")),))
            self.tap("b_kb", g["kb_b"][:], wait=(("sc", self.now("sc")),))
            self.tap("b_vc65", g["vc65_b"][:], wait=(("ve", self.now("ve")),))
            self.tap("b_bcol", g["fcol"][:, 0:32], wait=(("ve", self.now("ve")),))
        self.attn_b_side(0)
        if c == 0:
            self.tap("b_qfb", g["qfb_b"][:], wait=(("ve", self.now("ve")),))
        self.attn_b_side(1)
        if c == 0:
            self.tap("b_kfb", g["kfb_b"][:], wait=(("ve", self.now("ve")),))
        if getattr(self, "b_stop", None) == "sideu":
            return
        self.attn_b_side_T(0)
        self.attn_b_side_T(1)
        if getattr(self, "b_stop", None) == "sideT":
            return
        nheads = getattr(self, "b_nheads", H)
        if nheads == 0:
            return
        self.attn_b_head_pre(0, c)
        self.attn_b_head_mask(0)
        for h in range(nheads):
            if h + 1 < nheads:
                self.attn_b_head_pre(h + 1, c)
            self.attn_b_head_post(h, c)
            if h + 1 < nheads:
                self.attn_b_head_mask(h + 1)
        if c == 0:
            self.tap("b_ob", g["ob_b"][:], wait=(("ve", self.now("ve")),))
            self.tap("b_S", g["sA"][:, 0:1040], wait=(("ve", self.now("ve")),))
        self.emit("vector", lambda e: e.tensor_copy(out=g["Sb16_b"][:], in_=g["sA"][:, 0:1040]),
                  wait=(("ve", self.now("ve")),), inc="ve")
        self.hzset("sb16_ve", "ve")
        if getattr(self, "b_stop", None) == "heads":
            return
        wtr = (("ve", self.now("ve")),) + self.free_wait("pbf")
        for i in range(NIC):
            self.emit("tensor", lambda e, i=i: e.transpose(
                p["pbf"][:, 128 * i:128 * i + 128], g["ob_b"][:, 128 * i:128 * i + 128],
                g["ident"][:]), wait=wtr if i == 0 else (), inc="tn" if i == NIC - 1 else None)
        self.emit("vector", lambda e: e.tensor_copy(out=g["obT_b"][:], in_=p["pbf"][:]),
                  wait=(("tn", self.now("tn")),), inc="ve")
        self.set_free("pbf", "ve")
        wo0 = (("ve", self.now("ve")),) + self.hzw("atf0") + self.hzw("atf1")
        for oh in range(2):
            bank = p["pb%d" % oh]
            for ic in range(NIC):
                self.emit("tensor", lambda e, oh=oh, ic=ic, bank=bank: e.matmul(
                    bank[:], g["obT_b"][:, 128 * ic:128 * ic + 128],
                    g["warena"][:, 24576 + 1024 * ic + 512 * oh:24576 + 1024 * ic + 512 * oh + 512],
                    start=(ic == 0), stop=False),
                    wait=wo0 if (oh == 0 and ic == 0) else ())
            self.emit("tensor", lambda e, oh=oh, bank=bank: e.matmul(
                bank[:], g["onesq_b"][0:1, :],
                g["brow_b"][0:1, 1024 + 512 * oh:1024 + 512 * oh + 512],
                start=False, stop=True), inc="tn")
        self.hzset("abT_tn", "tn")
        wres = (("tn", self.now("tn")),)
        for oh in range(2):
            self.emit("vector", lambda e, oh=oh, resc=resc: e.tensor_add(
                out=resc[:, 512 * oh:512 * oh + 512], in0=resc[:, 512 * oh:512 * oh + 512],
                in1=p["pb%d" % oh][:]), wait=wres if oh == 0 else (),
                inc="ve" if oh == 1 else None)

    def phase_store(self):
        g = self.sb
        yv = self.dram["y"].rearrange("(n p) d -> p n d", p=128)
        self.emit("sync", lambda e: e.dma_start(
            out=yv, in_=g["res"][:].rearrange("p (n d) -> p n d", n=NCH)),
            wait=(("ve", self.now("ve")), ("sc", self.now("sc")), ("tn", self.now("tn")),
                  ("gp", self.now("gp"))), inc="dO", n=16)
        self.emit("sync", lambda e: e.wait_ge(self.semh["dO"], self.now("dO")))
        if self.now("dT"):
            self.emit("sync", lambda e, v=self.now("dT"): e.wait_ge(self.semh["dT"], v))

    def finalize(self):
        blk = self.stack.enter_context(self.nc.Block())
        for ename in ENGS:
            dec = getattr(blk, ename)
            ops = list(self.q[ename])
            def body(eng, ops=ops):
                for f in ops:
                    f(eng)
            dec(body)


def build(stack, debug_taps=(), stop_after=None, nchunks=NCH, detect=False,
          b_stop=None, b_nchunks=NCH, b_nheads=16, b_substop=None):
    nc = bass.Bass("TRN2", target_bir_lowering=False, detect_race_conditions=detect)
    kb = KB(nc, stack, debug_taps=debug_taps, stop_after=stop_after)
    kb.nchunks = nchunks
    kb.b_stop = b_stop
    kb.b_nchunks = b_nchunks
    kb.b_nheads = b_nheads
    kb.b_substop = b_substop
    kb.declare()
    kb.phase_load()
    if stop_after != "load":
        kb.phase_wquant()
        kb.phase_attn_a()
        if stop_after != "attn_a":
            kb.phase_ffn(1)
            if stop_after != "ffn1":
                kb.phase_attn_b()
                if stop_after != "attn_b":
                    kb.phase_ffn(2)
    kb.phase_store()
    kb.finalize()
    return nc, kb


from concourse.bass_utils import run_bass_kernel_spmd
import contextlib


def kernel(**inputs):
    X = np.ascontiguousarray(np.asarray(inputs["L0_attn_a_x"], np.float32))
    hm = host_inputs({k: np.asarray(v, np.float32) for k, v in inputs.items()})
    with contextlib.ExitStack() as stack:
        nc, kb = build(stack)
        in_maps = []
        for core in range(8):
            m = dict(hm)
            m["x"] = np.ascontiguousarray(X[core])
            in_maps.append(m)
        r = run_bass_kernel_spmd(nc, in_maps, core_ids=list(range(8)))
        out = np.stack([np.asarray(r.results[core]["y"], np.float32)
                        for core in range(8)], axis=0)
    return out
